# version 15
# baseline (speedup 1.0000x reference)
import numpy as np
import ml_dtypes
import concourse.bass as bass
import concourse.mybir as mybir
from concourse.bass_utils import run_bass_kernel_spmd

F32 = mybir.dt.float32
BF16 = mybir.dt.bfloat16
AF = mybir.ActivationFunctionType
ALU = mybir.AluOpType
bfdt = ml_dtypes.bfloat16

D = 4096
S = 4096
NH = 32
KVH = 8
HD = 128
CH = 512
NCHUNK = 8
TOWN = 1024
RANK = 32
NLORA = 16
NRTOT = NLORA * RANK
SHIFT = 12.0
EMPTY = -30000.0
SCALE = float(1.0 / np.sqrt(128.0))
EPS = 1e-5

VISITS = [(0, p) for p in range(4)] + [(1, p) for p in range(8)]
DIAG = {3, 11}


class Ctr:
    def __init__(self, step=1):
        self.v = 0
        self.step = step

    def tick(self):
        self.v += self.step
        return self.v


class Rec:

    def __init__(self, sem, ctr):
        self.sem = sem
        self.ctr = ctr
        self.plan = []
        self.marks = {}

    def wait(self, sem, val):
        if val is None or val <= 0:
            return
        k = id(sem)
        if self.marks.get(k, 0) >= val:
            return
        self.marks[k] = val

        def f(eng, s=sem, v=val):
            eng.wait_ge(s, v)

        self.plan.append(f)

    def waitd(self, dep):
        if dep is None:
            return
        if isinstance(dep, list):
            for d in dep:
                self.waitd(d)
            return
        self.wait(dep[0], dep[1])

    def frontier(self):
        return [(self.sem, self.ctr.v)]

    def call(self, meth, *a, inc=True, **kw):
        if inc:
            v = self.ctr.tick()

            def f(eng, m=meth, aa=a, kk=kw, s=self.sem, st=self.ctr.step):
                getattr(eng, m)(*aa, **kk).then_inc(s, st)
        else:
            v = self.ctr.v

            def f(eng, m=meth, aa=a, kk=kw):
                getattr(eng, m)(*aa, **kk)

        self.plan.append(f)
        return v


class DmaRec(Rec):

    def __init__(self, sems):
        super().__init__(sems[0], Ctr(16))
        self.sems = sems
        self.ctrs = [Ctr(16) for _ in sems]
        self.rr = 0

    def frontier(self):
        return [(s, c.v) for s, c in zip(self.sems, self.ctrs)]

    def dma(self, **kw):
        i = self.rr % len(self.sems)
        self.rr += 1
        sem, ctr = self.sems[i], self.ctrs[i]
        self.wait(sem, ctr.v)
        v = ctr.tick()

        def f(eng, kk=kw, s=sem, vv=v):
            eng.dma_start(**kk).then_inc(s, 16)

        self.plan.append(f)
        return (sem, v)


def build_program():
    nc = bass.Bass("TRN2", target_bir_lowering=False, debug=False)
    dram = nc.dram_tensor

    i_xTkv = dram("xTkv", [D, S], F32, kind="ExternalInput").ap()
    i_xown = dram("xown", [TOWN, D], F32, kind="ExternalInput").ap()
    i_wqT = dram("wqT", [D, D], F32, kind="ExternalInput").ap()
    i_wkT = dram("wkT", [D, KVH * HD], F32, kind="ExternalInput").ap()
    i_wvT = dram("wvT", [D, KVH * HD], F32, kind="ExternalInput").ap()
    i_woT = dram("woT", [D, D], F32, kind="ExternalInput").ap()
    i_kern = dram("kern", [D, D], F32, kind="ExternalInput").ap()
    i_bias = dram("biasT", [128, D], F32, kind="ExternalInput").ap()
    i_aall = dram("aallT", [D, NRTOT], F32, kind="ExternalInput").ap()
    i_ball = dram("ball", [NRTOT, D], F32, kind="ExternalInput").ap()
    i_asT = dram("asT", [D, RANK], F32, kind="ExternalInput").ap()
    i_bsT = dram("bsT", [RANK, D], F32, kind="ExternalInput").ap()
    i_gnr = dram("gatenr", [128, 8], F32, kind="ExternalInput").ap()
    i_cos_kv = dram("cos_kv", [128, S], F32, kind="ExternalInput").ap()
    i_sin_kv = dram("sin_kv", [128, S], F32, kind="ExternalInput").ap()
    i_cos_q = dram("cos_q", [128, TOWN], F32, kind="ExternalInput").ap()
    i_sin_q = dram("sin_q", [128, TOWN], F32, kind="ExternalInput").ap()
    i_cm4 = dram("cmask4", [128, 4 * CH], BF16, kind="ExternalInput").ap()
    i_btab = dram("btab", [128, 12], F32, kind="ExternalInput").ap()
    i_onesf = dram("onesf", [1, 128], F32, kind="ExternalInput").ap()
    i_onesb = dram("onesb", [128, 1], BF16, kind="ExternalInput").ap()
    i_onesbr = dram("onesbr", [1, 128], BF16, kind="ExternalInput").ap()
    i_epsc = dram("epsc", [128, 1], F32, kind="ExternalInput").ap()
    o_out = dram("out", [TOWN, D], F32, kind="ExternalOutput").ap()

    sc_xkv = dram("sc_xkv", [D, S], BF16, kind="Internal").ap()
    sc_q = dram("sc_q", [NH, 128, TOWN], BF16, kind="Internal").ap()
    sc_k = dram("sc_k", [KVH // 2, NCHUNK, 2, 128, CH], BF16, kind="Internal").ap()
    sc_v = dram("sc_v", [KVH, NCHUNK, CH, HD], BF16, kind="Internal").ap()
    sc_at = dram("sc_at", [TOWN, D], BF16, kind="Internal").ap()

    from contextlib import ExitStack
    stack = ExitStack()
    e = stack.enter_context

    NQ = 5
    gd_sems = [e(nc.semaphore(f"s_gd{i}")) for i in range(NQ)]
    sy_sems = [e(nc.semaphore(f"s_sy{i}")) for i in range(NQ)]
    s_pe = e(nc.semaphore("s_pe"))
    s_ve = e(nc.semaphore("s_ve"))
    s_ac = e(nc.semaphore("s_ac"))
    s_bar = e(nc.semaphore("s_bar"))

    gd = DmaRec(gd_sems)
    sy = DmaRec(sy_sems)
    pe = Rec(s_pe, Ctr())
    ve = Rec(s_ve, Ctr())
    ac = Rec(s_ac, Ctr())
    bar_n = [0]

    def barrier(gd_front=None, sy_front=None, skip_gd=False):
        for r in (pe, ve, ac):
            for sem, v in r.frontier():
                sy.wait(sem, v)
        for sem, v in (gd_front if gd_front is not None else gd.frontier()):
            sy.wait(sem, v)
        for sem, v in (sy_front if sy_front is not None else sy.frontier()):
            sy.wait(sem, v)
        bar_n[0] += 1

        def f(eng):
            eng.sem_inc(s_bar, 1)

        sy.plan.append(f)
        sy.marks[id(s_bar)] = bar_n[0]
        for r in ((pe, ve, ac) if skip_gd else (pe, ve, ac, gd)):
            r.wait(s_bar, bar_n[0])

    sb_onesf = e(nc.sbuf_tensor("p_onesf", [1, 128], F32))
    sb_onesb = e(nc.sbuf_tensor("p_onesb", [128, 1], BF16))
    sb_onesbr = e(nc.sbuf_tensor("p_onesbr", [1, 128], BF16))
    sb_btab = e(nc.sbuf_tensor("p_btab", [128, 12], F32))
    sb_gnr = e(nc.sbuf_tensor("p_gnr", [128, 8], F32))
    sb_cpool = e(nc.sbuf_tensor("p_cpool", [128, 1], F32))
    sb_axg = e(nc.sbuf_tensor("p_axg", [128, 4 * TOWN], BF16))
    sb_axs = e(nc.sbuf_tensor("p_axs", [RANK, TOWN], BF16))
    nc.const_aps.register(F32, EPS, sb_cpool[:, 0:1])

    g_base = []
    for dst, src in ((sb_onesf, i_onesf), (sb_onesb, i_onesb), (sb_onesbr, i_onesbr),
                     (sb_btab, i_btab), (sb_gnr, i_gnr), (sb_cpool, i_epsc)):
        g_base.append(gd.dma(out=dst[:], in_=src))

    def fview(t):
        return t.rearrange("(a p) t -> p a t", p=128)

    def ln_pass(xT_in, ntok, sc_out, tag, hook=None, post_hook=None):
        nch = ntok // CH
        SLAB = 4 * CH
        with ExitStack() as ps:
            pp = ps.enter_context
            xch = [pp(nc.sbuf_tensor(f"l{tag}_x{i}", [128, 32 * CH], BF16)) for i in range(3)]
            sq = [pp(nc.sbuf_tensor(f"l{tag}_q{i}", [128, CH], BF16)) for i in range(2)]
            tb = [pp(nc.sbuf_tensor(f"l{tag}_t{i}", [128, CH], BF16)) for i in range(2)]
            srow = [pp(nc.sbuf_tensor(f"l{tag}_s{i}", [1, CH], F32)) for i in range(2)]
            qrow = [pp(nc.sbuf_tensor(f"l{tag}_ss{i}", [1, CH], F32)) for i in range(2)]
            rrow = pp(nc.sbuf_tensor(f"l{tag}_r", [1, CH], F32))
            nrow = [pp(nc.sbuf_tensor(f"l{tag}_n{i}", [1, CH], BF16)) for i in range(2)]
            rbrow = [pp(nc.sbuf_tensor(f"l{tag}_rb{i}", [1, CH], BF16)) for i in range(2)]
            bcm = pp(nc.sbuf_tensor(f"l{tag}_bm", [128, CH], BF16))
            bcr = pp(nc.sbuf_tensor(f"l{tag}_br", [128, CH], BF16))
            p_sum = pp(nc.psum_tensor(f"l{tag}_ps", [1, CH], F32))
            p_sq = pp(nc.psum_tensor(f"l{tag}_pq", [1, CH], F32))
            p_bc = pp(nc.psum_tensor(f"l{tag}_pb", [128, 2 * CH], F32))
            xload = [None, None, None]
            xfree = [0, 0, 0]
            xttl = [None, None, None]
            xpe = [0, 0, 0]
            xstore = [None, None, None]
            qdep = [0, 0]
            tdep = [0, 0]
            row_done = 0
            rowfin = [0, 0]
            rowpe = [0, 0]
            rowac = [0]
            stats_done = {}

            def do_stats(tc):
                nonlocal row_done
                xi = tc % 3
                ri = tc % 2
                pe.wait(s_ve, row_done)
                for dc in range(32):
                    si = dc % 2
                    pe.waitd(xload[xi][dc // 8])
                    ac.waitd(xload[xi][dc // 8])
                    ac.wait(s_pe, qdep[si])
                    av = ac.call("activation", sq[si][:],
                                 xch[xi][:, dc * CH:(dc + 1) * CH], AF.Square)
                    pe.wait(s_ac, av)
                    pe.call("matmul", p_sum[0:1, :], sb_onesb[:, 0:1],
                            xch[xi][:, dc * CH:(dc + 1) * CH],
                            start=(dc == 0), stop=(dc == 31))
                    pq = pe.call("matmul", p_sq[0:1, :], sb_onesb[:, 0:1],
                                 sq[si][:],
                                 start=(dc == 0), stop=(dc == 31))
                    qdep[si] = pq
                stats_done[tc] = pe.ctr.v
                ve.wait(s_pe, stats_done[tc])
                ve.wait(s_ve, ve.ctr.v)
                ve.call("tensor_copy", srow[ri][0:1, :], p_sum[0:1, :])
                row_done = ve.call("tensor_copy", qrow[ri][0:1, :], p_sq[0:1, :])

            def do_rows(tc):
                ri = tc % 2
                ve.wait(s_ve, ve.ctr.v)
                ve.wait(s_ac, rowac[0])
                m1 = ve.call("tensor_scalar", srow[ri][0:1, :], srow[ri][0:1, :],
                             1.0 / D, None, ALU.mult)
                ve.wait(s_ve, m1)
                v1 = ve.call("tensor_mul", rrow[0:1, :], srow[ri][0:1, :],
                             srow[ri][0:1, :])
                ve.wait(s_ve, v1)
                v2 = ve.call("scalar_tensor_tensor", rrow[0:1, :], qrow[ri][0:1, :],
                             1.0 / D, rrow[0:1, :], ALU.mult, ALU.subtract)
                ac.wait(s_ve, v2)
                ac.waitd(g_base)
                a1 = ac.call("activation", rrow[0:1, :], rrow[0:1, :], AF.Ln, bias=EPS)
                ac.wait(s_ac, a1)
                a2 = ac.call("activation", rrow[0:1, :], rrow[0:1, :], AF.Exp,
                             scale=-0.5)
                rowac[0] = a2
                ve.wait(s_ac, a2)
                ve.wait(s_pe, rowpe[ri])
                ve.call("tensor_scalar", nrow[ri][0:1, :], srow[ri][0:1, :], -1.0,
                        None, ALU.mult)
                n2 = ve.call("tensor_copy", rbrow[ri][0:1, :], rrow[0:1, :])
                rowfin[ri] = n2

            def do_tts(tc):
                xi = tc % 3
                ri = tc % 2
                pe.wait(s_ve, rowfin[ri])
                pe.call("matmul", p_bc[:, 0:CH], sb_onesbr[0:1, :],
                        nrow[ri][0:1, :], start=True, stop=True)
                pb = pe.call("matmul", p_bc[:, CH:2 * CH], sb_onesbr[0:1, :],
                             rbrow[ri][0:1, :], start=True, stop=True)
                rowpe[ri] = pb
                ve.wait(s_pe, pb)
                ve.call("tensor_copy", bcm[:], p_bc[:, 0:CH])
                bcv = ve.call("tensor_copy", bcr[:], p_bc[:, CH:2 * CH])
                ve.wait(s_ve, bcv)
                last_tt = 0
                stores = []
                ttl = []
                for dc in range(32):
                    ti = dc % 2
                    ve.wait(s_ve, tdep[ti])
                    xs = xch[xi][:, dc * CH:(dc + 1) * CH]
                    v1 = ve.call("tensor_add", tb[ti][:], xs, bcm[:])
                    ve.wait(s_ve, v1)
                    v2 = ve.call("tensor_mul", xs, tb[ti][:], bcr[:])
                    tdep[ti] = v2
                    last_tt = v2
                    ttl.append(v2)
                    if dc % 2 == 1:
                        sy.wait(s_ve, v2)
                        stores.append(sy.dma(
                            out=fview(sc_out)[:, dc - 1:dc + 1,
                                              tc * CH:(tc + 1) * CH],
                            in_=xch[xi][:, (dc - 1) * CH:(dc + 1) * CH].rearrange(
                                "p (a t) -> p a t", a=2)))
                xfree[xi] = last_tt
                xttl[xi] = ttl
                xstore[xi] = stores

            def load_chunk(tc):
                xi = tc % 3
                gd.wait(s_ve, xfree[xi])
                gd.wait(s_pe, xpe[xi])
                gd.waitd(xstore[xi])
                deps = []
                for pc in range(4):
                    deps.append(gd.dma(
                        out=xch[xi][:].rearrange("p (a t) -> p a t", a=32)[
                            :, pc * 8:(pc + 1) * 8, :],
                        in_=fview(xT_in)[:, pc * 8:(pc + 1) * 8,
                                         tc * CH:(tc + 1) * CH]))
                xload[xi] = deps

            load_chunk(0)
            for tc in range(nch):
                if hook is not None:
                    hook(tc)
                do_stats(tc)
                do_rows(tc)
                if tc > 0:
                    do_tts(tc - 1)
                    if post_hook is not None:
                        xpe[(tc - 1) % 3] = post_hook(tc - 1, xch[(tc - 1) % 3],
                                                      xttl[(tc - 1) % 3])
                if tc + 1 < nch:
                    load_chunk(tc + 1)
            do_tts(nch - 1)
            if post_hook is not None:
                xpe[(nch - 1) % 3] = post_hook(nch - 1, xch[(nch - 1) % 3],
                                               xttl[(nch - 1) % 3])

    def rope_block(b0, b1, cs, sn, bufs, dst0, dst1):
        er, orr, t1, t2, ke, ko = bufs
        ve.call("tensor_copy", er[:], b0[:])
        c2 = ve.call("tensor_copy", orr[:], b1[:])
        ve.wait(s_ve, c2)
        ve.call("tensor_mul", t1[:], er[:], cs)
        v2 = ve.call("scalar_tensor_tensor", t2[:], orr[:], -1.0, sn, ALU.mult, ALU.mult)
        ve.wait(s_ve, v2)
        v3 = ve.call("tensor_add", ke[:], t1[:], t2[:])
        ve.wait(s_ve, v3)
        ve.call("tensor_mul", t1[:], orr[:], cs)
        v5 = ve.call("tensor_mul", t2[:], er[:], sn)
        ve.wait(s_ve, v5)
        v6 = ve.call("tensor_add", ko[:], t1[:], t2[:])
        sy.wait(s_ve, v3)
        s1 = sy.dma(out=dst0, in_=ke[:])
        sy.wait(s_ve, v6)
        s2 = sy.dma(out=dst1, in_=ko[:])
        return c2, v6, [s1, s2]

    def kv_sweeps(wsb, kscr, pk, xp, xqh0, xqh1, wv_deps, xv0):
        with ExitStack() as ps:
            pv4 = [ps.enter_context(nc.psum_tensor(f"v_p{i}", [128, CH], F32))
                   for i in range(4)]
            bufs = [xp[0], xqh0, xqh1]
            idxmap = [0, 2, 0, 1, 0, 2, 0, 2]
            xdep = [0, 0, 0]
            xv = [None] * NCHUNK
            xv[0] = xv0
            sdep = [None, None]
            vstage = [kscr[4], kscr[5]]
            pdep = [0, 0]

            def loadx(tc):
                bi = idxmap[tc]
                sy.wait(s_pe, xdep[bi])
                xv[tc] = sy.dma(out=bufs[bi][:],
                                in_=fview(sc_xkv)[:, :, tc * CH:(tc + 1) * CH])

            loadx(1)
            pall = [pk[0], pk[1], pk[2], pk[3], pv4[0], pv4[1], pv4[2], pv4[3]]
            pe.waitd(xv[0])
            for dc in range(32):
                if dc % 4 == 0:
                    pe.waitd(wv_deps[dc // 4])
                for b in range(8):
                    tt, vh = b // 2, b % 2
                    pe.call("matmul", pall[b][:],
                            bufs[0][:, dc * CH + tt * 128:dc * CH + (tt + 1) * 128],
                            wsb[:, dc * 1024 + vh * CH:dc * 1024 + (vh + 1) * CH],
                            start=(dc == 0), stop=(dc == 31),
                            inc=(dc == 31 and b == 7))
            pv = pe.ctr.v
            xdep[0] = pv
            for b in range(8):
                tt, vh = b // 2, b % 2
                par = b % 2
                ve.wait(s_pe, pv)
                ve.waitd(sdep[par])
                cv = ve.call("tensor_copy", vstage[par][:], pall[b][:])
                if b < 2:
                    pdep[b] = cv
                sy.wait(s_ve, cv)
                svl = []
                for hh in range(4):
                    h = vh * 4 + hh
                    svl.append(sy.dma(
                        out=sc_v[h, 0, tt * 128:(tt + 1) * 128, :],
                        in_=vstage[par][:, hh * 128:(hh + 1) * 128]))
                sdep[par] = svl
            for tc in range(1, NCHUNK):
                bi = idxmap[tc]
                if tc + 1 < NCHUNK:
                    loadx(tc + 1)
                pe.waitd(xv[tc])
                for tt in range(4):
                    for vh in range(2):
                        par = vh % 2
                        pb = pk[par]
                        pe.wait(s_ve, pdep[par])
                        for dc in range(32):
                            pe.call("matmul", pb[:],
                                    bufs[bi][:, dc * CH + tt * 128:
                                             dc * CH + (tt + 1) * 128],
                                    wsb[:, dc * 1024 + vh * CH:
                                        dc * 1024 + (vh + 1) * CH],
                                    start=(dc == 0), stop=(dc == 31), inc=(dc == 31))
                        pv = pe.ctr.v
                        xdep[bi] = pv
                        ve.wait(s_pe, pv)
                        ve.waitd(sdep[par])
                        cv = ve.call("tensor_copy", vstage[par][:], pb[:])
                        pdep[par] = cv
                        sy.wait(s_ve, cv)
                        svl = []
                        for hh in range(4):
                            h = vh * 4 + hh
                            svl.append(sy.dma(
                                out=sc_v[h, tc, tt * 128:(tt + 1) * 128, :],
                                in_=vstage[par][:, hh * 128:(hh + 1) * 128]))
                        sdep[par] = svl



    def q_proj(xqh0, xqh1, pk, kcs, ksn, kscr, aall, asb, lg):
        with ExitStack() as ps:
            pp = ps.enter_context
            wq = [pp(nc.sbuf_tensor(f"q_w{i}", [128, 32 * 128], BF16))
                  for i in range(2)]
            xqh = [xqh0, xqh1]
            cd = [gd.dma(out=kcs[t][:], in_=i_cos_q[:, t * CH:(t + 1) * CH])
                  for t in range(2)]
            sd = [gd.dma(out=ksn[t][:], in_=i_sin_q[:, t * CH:(t + 1) * CH])
                  for t in range(2)]
            ve.waitd(cd)
            ve.waitd(sd)
            wdep = [0, 0]
            pdep = [0, 0, 0, 0]
            sdep = None
            rdep = 0
            gv = [None, None]
            gv[0] = gd.dma(out=wq[0][:].rearrange("p (a o) -> p a o", a=32),
                           in_=fview(i_wqT)[:, :, 0:128])
            for pr in range(NH // 2):
                pvt = [0, 0]
                for idx in range(2):
                    hp = 2 * pr + idx
                    wi = hp % 2
                    if hp + 1 < NH:
                        ni = (hp + 1) % 2
                        gd.wait(s_pe, wdep[ni])
                        gv[ni] = gd.dma(
                            out=wq[ni][:].rearrange("p (a o) -> p a o", a=32),
                            in_=fview(i_wqT)[:, :, (hp + 1) * 128:(hp + 2) * 128])
                    if hp == 4:
                        lg.append(gd.dma(out=aall[:].rearrange("p (a r) -> p a r", a=32),
                                         in_=fview(i_aall)))
                        lg.append(gd.dma(out=asb[:].rearrange("p (a r) -> p a r", a=32),
                                         in_=fview(i_asT)))
                    pe.waitd(gv[wi])
                    for tc in range(2):
                        pb = pk[2 * tc + idx]
                        pe.wait(s_ve, pdep[2 * tc + idx])
                        for dc in range(32):
                            pe.call("matmul", pb[:],
                                    wq[wi][:, dc * 128:(dc + 1) * 128],
                                    xqh[tc][:, dc * CH:(dc + 1) * CH],
                                    start=(dc == 0), stop=(dc == 31), inc=(dc == 31))
                        pvt[tc] = pe.ctr.v
                    wdep[wi] = pe.ctr.v
                for tc in range(2):
                    b0, b1 = pk[2 * tc], pk[2 * tc + 1]
                    ve.wait(s_pe, pvt[tc])
                    ve.wait(s_ve, rdep)
                    ve.waitd(sdep)
                    c2, v6, s2 = rope_block(
                        b0, b1, kcs[tc][:], ksn[tc][:], kscr[0:6],
                        sc_q[2 * pr, :, tc * CH:(tc + 1) * CH],
                        sc_q[2 * pr + 1, :, tc * CH:(tc + 1) * CH])
                    pdep[2 * tc] = c2
                    pdep[2 * tc + 1] = c2
                    rdep = v6
                    sdep = s2

    def lora_pre(xqh0, xqh1, aall, asb, lg):
        with ExitStack() as ps:
            pp = ps.enter_context
            pl = pp(nc.psum_tensor("lp_pl", [128, CH], F32))
            pls = pp(nc.psum_tensor("lp_pls", [RANK, CH], F32))
            xqh = [xqh0, xqh1]
            pe.waitd(lg)
            for nt in range(4):
                for tc in range(2):
                    for dc in range(32):
                        pe.call("matmul", pl[:],
                                aall[:, dc * NRTOT + nt * 128:dc * NRTOT + (nt + 1) * 128],
                                xqh[tc][:, dc * CH:(dc + 1) * CH],
                                start=(dc == 0), stop=(dc == 31), inc=(dc == 31))
                    pv = pe.ctr.v
                    ve.wait(s_pe, pv)
                    vv = ve.call("tensor_scalar",
                                 sb_axg[:, nt * TOWN + tc * CH:nt * TOWN + (tc + 1) * CH],
                                 pl[:], sb_gnr[:, tc * 4 + nt:tc * 4 + nt + 1], None,
                                 ALU.mult)
                    pe.wait(s_ve, vv)
            for tc in range(2):
                for dc in range(32):
                    pe.call("matmul", pls[:],
                            asb[:, dc * RANK:(dc + 1) * RANK],
                            xqh[tc][:, dc * CH:(dc + 1) * CH],
                            start=(dc == 0), stop=(dc == 31), inc=(dc == 31))
                pv = pe.ctr.v
                ve.wait(s_pe, pv)
                vv = ve.call("tensor_copy", sb_axs[:, tc * CH:(tc + 1) * CH], pls[:])
                pe.wait(s_ve, vv)

    with ExitStack() as ps_lnkv:
        plk = ps_lnkv.enter_context
        wsb = plk(nc.sbuf_tensor("kv_w", [128, 32 * KVH * HD], BF16))
        kcs = [plk(nc.sbuf_tensor(f"kc{i}", [128, CH], BF16)) for i in range(2)]
        ksn = [plk(nc.sbuf_tensor(f"ks{i}", [128, CH], BF16)) for i in range(2)]
        kscr = [plk(nc.sbuf_tensor(f"kscr{i}", [128, CH], BF16)) for i in range(10)]
        pk = [plk(nc.psum_tensor(f"kp{i}", [128, CH], F32)) for i in range(4)]
        wk_deps = []
        cs_deps = [None] * NCHUNK
        kstate = {"pdep": [0, 0, 0, 0], "sdep": [None, None], "rdep": 0,
                  "edep": 0, "csfree": [0, 0], "wkfree": [0] * 8}

        def wk_hook(tc):
            if tc < 2:
                for piece in range(4 * tc, 4 * tc + 4):
                    wk_deps.append(gd.dma(
                        out=wsb[:].rearrange("p (a o) -> p a o", a=32)[
                            :, piece * 4:(piece + 1) * 4, :],
                        in_=fview(i_wkT)[:, piece * 4:(piece + 1) * 4, :]))
            if tc < NCHUNK:
                ci = tc % 2
                gd.wait(s_ve, kstate["csfree"][ci])
                cs_deps[tc] = [
                    gd.dma(out=kcs[ci][:], in_=i_cos_kv[:, tc * CH:(tc + 1) * CH]),
                    gd.dma(out=ksn[ci][:], in_=i_sin_kv[:, tc * CH:(tc + 1) * CH])]

        def rope_pair(pv, half, tc, ci):
            er0, orr0, er1, orr1, t1, t2, ke0, ko0, ke1, ko1 = kscr
            ve.wait(s_pe, pv)
            ve.wait(s_ve, kstate["edep"])
            ve.call("tensor_copy", er0[:], pk[0][:])
            ve.call("tensor_copy", orr0[:], pk[1][:])
            ve.call("tensor_copy", er1[:], pk[2][:])
            c2 = ve.call("tensor_copy", orr1[:], pk[3][:])
            for g in range(4):
                kstate["pdep"][g] = c2
            ve.wait(s_ve, c2)
            cs, sn = kcs[ci][:], ksn[ci][:]
            vlast = 0
            for g2, (er, orr, ke, ko) in enumerate(
                    ((er0, orr0, ke0, ko0), (er1, orr1, ke1, ko1))):
                pr = 2 * half + g2
                ve.wait(s_ve, vlast)
                ve.call("tensor_mul", t1[:], er[:], cs)
                v2 = ve.call("scalar_tensor_tensor", t2[:], orr[:], -1.0, sn,
                             ALU.mult, ALU.mult)
                ve.wait(s_ve, v2)
                ve.waitd(kstate["sdep"][g2])
                v3 = ve.call("tensor_add", ke[:], t1[:], t2[:])
                ve.wait(s_ve, v3)
                ve.call("tensor_mul", t1[:], orr[:], cs)
                v5 = ve.call("tensor_mul", t2[:], er[:], sn)
                ve.wait(s_ve, v5)
                v6 = ve.call("tensor_add", ko[:], t1[:], t2[:])
                sy.wait(s_ve, v3)
                s1 = sy.dma(out=sc_k[pr, tc, 0], in_=ke[:])
                sy.wait(s_ve, v6)
                s2 = sy.dma(out=sc_k[pr, tc, 1], in_=ko[:])
                kstate["sdep"][g2] = [s1, s2]
                vlast = v6
            kstate["rdep"] = vlast
            kstate["edep"] = vlast

        def k_hook(tc, xch_t, ttlist):
            ci = tc % 2
            pe.waitd(wk_deps)
            ve.waitd(cs_deps[tc])
            for half in range(2):
                for dc in range(32):
                    if half == 0:
                        pe.wait(s_ve, ttlist[dc])
                    for g in range(4):
                        if dc == 0:
                            pe.wait(s_ve, kstate["pdep"][g])
                        idx = 4 * half + g
                        lastwk = (tc == NCHUNK - 1 and half == 1 and g == 3
                                  and dc % 4 == 3)
                        pe.call("matmul", pk[g][:],
                                wsb[:, dc * 1024 + idx * 128:
                                    dc * 1024 + (idx + 1) * 128],
                                xch_t[:, dc * CH:(dc + 1) * CH],
                                start=(dc == 0), stop=(dc == 31),
                                inc=(dc == 31 and g == 3) or lastwk)
                        if lastwk:
                            kstate["wkfree"][dc // 4] = pe.ctr.v
                pv = pe.ctr.v
                rope_pair(pv, half, tc, ci)
            kstate["csfree"][ci] = kstate["rdep"]
            return pe.ctr.v

        ln_pass(i_xTkv, S, sc_xkv, "kv", hook=wk_hook, post_hook=k_hook)
        xqh0 = plk(nc.sbuf_tensor("q_x0", [128, 32 * CH], BF16))
        xqh1 = plk(nc.sbuf_tensor("q_x1", [128, 32 * CH], BF16))
        with ExitStack() as psv:
            xp = [psv.enter_context(nc.sbuf_tensor("kv_x0", [128, 32 * CH], BF16))]
            gdf = gd.frontier()
            syf = sy.frontier()
            wv_deps = []
            for piece in range(8):
                gd.wait(s_pe, kstate["wkfree"][piece])
                wv_deps.append(gd.dma(
                    out=wsb[:].rearrange("p (a o) -> p a o", a=32)[
                        :, piece * 4:(piece + 1) * 4, :],
                    in_=fview(i_wvT)[:, piece * 4:(piece + 1) * 4, :]))
            xv0 = sy.dma(out=xp[0][:], in_=fview(sc_xkv)[:, :, 0:CH])
            barrier(gd_front=gdf, sy_front=syf)
            kv_sweeps(wsb, kscr, pk, xp, xqh0, xqh1, wv_deps, xv0)
        aall = plk(nc.sbuf_tensor("lp_aall", [128, 32 * NRTOT], BF16))
        asb = plk(nc.sbuf_tensor("lp_as", [128, 32 * RANK], BF16))
        lg = []
        barrier()
        q_proj(xqh0, xqh1, pk, kcs, ksn, kscr, aall, asb, lg)
        barrier()
        lora_pre(xqh0, xqh1, aall, asb, lg)
    syf_q = sy.frontier()
    barrier()


    with ExitStack() as ps67:
        pp = ps67.enter_context
        sb_attn = pp(nc.sbuf_tensor("p_attn", [128, NH * TOWN], BF16))
        wo = [pp(nc.sbuf_tensor(f"w_w{i}", [128, 32 * CH], BF16)) for i in range(2)]

        def attention():
            with ExitStack() as ps:
                p2 = ps.enter_context
                sb_cm4 = p2(nc.sbuf_tensor("p_cm4", [128, 4 * CH], BF16))
                gd.waitd(syf_q)
                cmld = gd.dma(out=sb_cm4[:], in_=i_cm4)
                ve.waitd(cmld)
                qsb = [p2(nc.sbuf_tensor(f"a_q{i}", [128, 2 * CH], BF16)) for i in range(2)]
                kb = [p2(nc.sbuf_tensor(f"a_k{i}", [128, CH], BF16)) for i in range(3)]
                vb = [p2(nc.sbuf_tensor(f"a_v{i}", [128, 4 * HD], BF16)) for i in range(3)]
                psb = [p2(nc.sbuf_tensor(f"a_p{i}", [128, 2 * CH], BF16)) for i in range(4)]
                prow = [p2(nc.sbuf_tensor(f"a_w{i}", [128, CH], BF16)) for i in range(4)]
                rcs = [p2(nc.sbuf_tensor(f"a_rc{i}", [1, CH], F32)) for i in range(2)]
                bcs = [p2(nc.sbuf_tensor(f"a_bc{i}", [128, CH], F32)) for i in range(2)]
                p_sc = [p2(nc.psum_tensor(f"a_s{i}", [128, 2 * CH], F32)) for i in range(2)]
                p_av = [p2(nc.psum_tensor(f"a_av{i}", [128, CH], F32)) for i in range(2)]
                p_rw = [p2(nc.psum_tensor(f"a_rw{i}", [128, CH], F32)) for i in range(2)]

                blocks = [(kvh, duo, qb)
                          for kvh in range(KVH) for duo in range(2) for qb in range(2)]
                sched = []
                for bi, (kvh, duo, qb) in enumerate(blocks):
                    vis = [v for v in range(12) if VISITS[v][0] == qb]
                    for vi, v in enumerate(vis):
                        sched.append((bi, vi, v))
                NG = len(sched)

                qdep = {}
                qld = {}
                kdep = [0, 0, 0]
                vdep = [0, 0, 0]
                scdep = [0, 0]
                pbdep = [0, 0, 0, 0]
                prdep = [0, 0, 0, 0]
                exv = {}
                exk = {}
                avdep = [0, 0]
                rwdep = [0, 0]
                rcsd = [0, 0]
                bcsd = [0, 0]
                kvst = {}

                def load_kv(kvh, pos, par):
                    gd.wait(s_pe, kdep[par])
                    kl = [gd.dma(out=kb[par][0:64, :],
                                 in_=sc_k[kvh // 2, pos, 0,
                                          64 * (kvh % 2):64 * (kvh % 2) + 64, :]),
                          gd.dma(out=kb[par][64:128, :],
                                 in_=sc_k[kvh // 2, pos, 1,
                                          64 * (kvh % 2):64 * (kvh % 2) + 64, :])]
                    gd.wait(s_pe, vdep[par])
                    vl = gd.dma(out=vb[par][:].rearrange("p (a d) -> p a d", a=4),
                                in_=sc_v[kvh, pos].rearrange("(a p) d -> p a d", p=128))
                    return kl, vl

                def load_q(bi):
                    kvh, duo, qb = blocks[bi]
                    qpar = bi % 2
                    qh = (4 * kvh + 2 * duo, 4 * kvh + 2 * duo + 1)
                    for z in range(2):
                        h = qh[z]
                        gd.wait(s_pe, qdep.get((z, qpar), 0))
                        qld[(z, qpar)] = [
                            gd.dma(out=qsb[z][0:64, qpar * CH:(qpar + 1) * CH],
                                   in_=sc_q[2 * (h // 2),
                                            64 * (h % 2):64 * (h % 2) + 64,
                                            qb * CH:(qb + 1) * CH]),
                            gd.dma(out=qsb[z][64:128, qpar * CH:(qpar + 1) * CH],
                                   in_=sc_q[2 * (h // 2) + 1,
                                            64 * (h % 2):64 * (h % 2) + 64,
                                            qb * CH:(qb + 1) * CH])]

                def scores(z, g, v, par, qpar, kl):
                    zz = 2 * z + g
                    pe.waitd(kl)
                    pe.waitd(qld[(z, qpar)])
                    pe.wait(s_ac, scdep[z])
                    for t in range(2):
                        pe.call("matmul", p_sc[z][:, t * CH:(t + 1) * CH],
                                kb[par][:, (2 * g + t) * 128:(2 * g + t + 1) * 128],
                                qsb[z][:, qpar * CH:(qpar + 1) * CH],
                                start=True, stop=True, inc=(t == 1))
                    spv = pe.ctr.v
                    qdep[(z, qpar)] = spv
                    ac.wait(s_pe, spv)
                    ac.wait(s_pe, pbdep[zz])
                    a = ac.call("activation", psb[zz][:], p_sc[z][:], AF.Exp,
                                scale=SCALE, bias=sb_btab[:, v:v + 1])
                    scdep[z] = a
                    if v in DIAG:
                        ve.wait(s_ac, a)
                        mv = 0
                        for t in range(2):
                            kt = 2 * g + t
                            mv = ve.call("tensor_mul",
                                         psb[zz][:, t * CH:(t + 1) * CH],
                                         psb[zz][:, t * CH:(t + 1) * CH],
                                         sb_cm4[:, kt * CH:(kt + 1) * CH])
                        exv[(z, g)] = mv
                        exk[(z, g)] = "ve"
                        ve.wait(s_ve, mv)
                    else:
                        exv[(z, g)] = a
                        exk[(z, g)] = "ac"
                        ve.wait(s_ac, a)
                    ve.wait(s_pe, prdep[zz])
                    pav = ve.call("tensor_add", prow[zz][:],
                                  psb[zz][:, 0:CH], psb[zz][:, CH:2 * CH])
                    exv[(z, g, "row")] = pav

                def avrows(z, g, par, vl, first, last):
                    zz = 2 * z + g
                    if exk[(z, g)] == "ve":
                        pe.wait(s_ve, exv[(z, g)])
                    else:
                        pe.wait(s_ac, exv[(z, g)])
                    pe.waitd(vl)
                    if first:
                        pe.wait(s_ve, avdep[z])
                        pe.wait(s_ve, rwdep[z])
                    for t in range(2):
                        kt = 2 * g + t
                        pe.call("matmul", p_av[z][:],
                                vb[par][:, kt * 128:(kt + 1) * 128],
                                psb[zz][:, t * CH:(t + 1) * CH],
                                start=(first and t == 0), stop=(last and t == 1),
                                inc=False)
                    pe.wait(s_ve, exv[(z, g, "row")])
                    pe.call("matmul", p_rw[z][0:1, :], sb_onesb[:, 0:1],
                            prow[zz][:], start=first, stop=last)
                    pbdep[zz] = pe.ctr.v
                    prdep[zz] = pe.ctr.v
                    kdep[par] = pe.ctr.v
                    vdep[par] = pe.ctr.v

                def finalize(bi):
                    kvh, duo, qb = blocks[bi]
                    qh = (4 * kvh + 2 * duo, 4 * kvh + 2 * duo + 1)
                    for z in range(2):
                        h = qh[z]
                        ve.wait(s_pe, pbdep[2 * z])
                        ve.wait(s_pe, pbdep[2 * z + 1])
                        ve.wait(s_pe, rcsd[z])
                        rv = ve.call("reciprocal", rcs[z][0:1, :], p_rw[z][0:1, :])
                        rwdep[z] = rv
                        pe.wait(s_ve, rv)
                        bv = pe.call("matmul", p_rw[z][:, :], sb_onesf[0:1, :],
                                     rcs[z][0:1, :], start=True, stop=True)
                        rcsd[z] = bv
                        ve.wait(s_pe, bv)
                        ve.wait(s_ve, bcsd[z])
                        cv = ve.call("tensor_copy", bcs[z][:], p_rw[z][:, :])
                        rwdep[z] = cv
                        ve.wait(s_ve, cv)
                        nv2 = ve.call("tensor_mul",
                                      sb_attn[:, h * TOWN + qb * CH:
                                              h * TOWN + (qb + 1) * CH],
                                      p_av[z][:], bcs[z][:])
                        avdep[z] = nv2
                        bcsd[z] = nv2

                load_q(0)
                kvst[0] = load_kv(blocks[0][0], VISITS[sched[0][2]][1], 0)
                for gi, (bi, vi, v) in enumerate(sched):
                    qpar = bi % 2
                    par = gi % 3
                    kl, vl = kvst[gi]
                    if gi + 1 < NG:
                        nbi, nvi, nv_ = sched[gi + 1]
                        kvst[gi + 1] = load_kv(blocks[nbi][0], VISITS[nv_][1],
                                               (gi + 1) % 3)
                        if nvi == 0:
                            load_q(nbi)
                    scores(0, 0, v, par, qpar, kl)
                    scores(1, 0, v, par, qpar, kl)
                    if gi > 0:
                        pvl = kvst[gi - 1][1]
                        ppar = (gi - 1) % 3
                        avrows(0, 1, ppar, pvl, False, vi == 0)
                        avrows(1, 1, ppar, pvl, False, vi == 0)
                    if vi == 0 and bi > 0:
                        finalize(bi - 1)
                    scores(0, 1, v, par, qpar, kl)
                    scores(1, 1, v, par, qpar, kl)
                    avrows(0, 0, par, vl, vi == 0, False)
                    avrows(1, 0, par, vl, vi == 0, False)
                gl = NG - 1
                avrows(0, 1, gl % 3, kvst[gl][1], False, True)
                avrows(1, 1, gl % 3, kvst[gl][1], False, True)
                finalize(len(blocks) - 1)

        attention()
        gdf = gd.frontier()
        gv0 = gd.dma(out=wo[0][:].rearrange("p (a o) -> p a o", a=32),
                     in_=fview(i_woT)[:, :, 0:CH])
        barrier(gd_front=gdf)

        def wo_phase(gv0):
            with ExitStack() as ps:
                p2 = ps.enter_context
                stg = [p2(nc.sbuf_tensor(f"w_s{i}", [128, CH], BF16)) for i in range(2)]
                pw = [p2(nc.psum_tensor(f"w_p{i}", [128, CH], F32)) for i in range(2)]
                wdep = [0, 0]
                pdep = [0, 0]
                sdep = [None, None]
                gv = [gv0, None]
                for dc in range(8):
                    wi = dc % 2
                    if dc + 1 < 8:
                        ni = (dc + 1) % 2
                        gd.wait(s_pe, wdep[ni])
                        gv[ni] = gd.dma(
                            out=wo[ni][:].rearrange("p (a o) -> p a o", a=32),
                            in_=fview(i_woT)[:, :, (dc + 1) * CH:(dc + 2) * CH])
                    pe.waitd(gv[wi])
                    for tt in range(8):
                        par = tt % 2
                        pb = pw[par]
                        pe.wait(s_ve, pdep[par])
                        for oc in range(32):
                            qb, tl = tt // 4, tt % 4
                            pe.call("matmul", pb[:],
                                    sb_attn[:, oc * TOWN + qb * CH + tl * 128:
                                            oc * TOWN + qb * CH + (tl + 1) * 128],
                                    wo[wi][:, oc * CH:(oc + 1) * CH],
                                    start=(oc == 0), stop=(oc == 31), inc=(oc == 31))
                        pv = pe.ctr.v
                        wdep[wi] = pv
                        ve.wait(s_pe, pv)
                        ve.waitd(sdep[par])
                        cv = ve.call("tensor_copy", stg[par][:], pb[:])
                        pdep[par] = cv
                        sy.wait(s_ve, cv)
                        sv = sy.dma(out=sc_at[tt * 128:(tt + 1) * 128,
                                              dc * CH:(dc + 1) * CH],
                                    in_=stg[par][:])
                        sdep[par] = sv

        wo_phase(gv0)
    barrier()

    def final_phase():
        with ExitStack() as ps:
            pp = ps.enter_context
            xq = pp(nc.sbuf_tensor("f_x", [128, 32 * TOWN], BF16))
            ballb = [pp(nc.sbuf_tensor(f"f_bl{i}", [128, 4 * CH], BF16)) for i in range(2)]
            bsb = [pp(nc.sbuf_tensor(f"f_bs{i}", [RANK, CH], BF16)) for i in range(2)]
            pf = [pp(nc.psum_tensor(f"f_p{i}", [128, CH], F32)) for i in range(2)]
            pl = pp(nc.psum_tensor("f_pl", [128, CH], F32))
            pls = pp(nc.psum_tensor("f_pls", [RANK, CH], F32))

            xv = [sy.dma(out=xq[:].rearrange("p (a t) -> p a t", a=32)[:, :, 0:CH],
                         in_=fview(sc_xkv)[:, :, 3 * CH:4 * CH]),
                  sy.dma(out=xq[:].rearrange("p (a t) -> p a t", a=32)[:, :, CH:2 * CH],
                         in_=fview(sc_xkv)[:, :, 7 * CH:8 * CH])]
            pe.waitd(xv)
            bias = pp(nc.sbuf_tensor("f_bias", [128, D], F32))
            kn = [pp(nc.sbuf_tensor(f"f_k{i}", [128, 32 * CH], BF16)) for i in range(2)]
            xc = pp(nc.sbuf_tensor("f_xc", [128, 4 * CH], F32))
            at = [pp(nc.sbuf_tensor(f"f_at{i}", [128, CH], BF16)) for i in range(2)]
            st1 = [pp(nc.sbuf_tensor(f"f_s1_{i}", [128, CH], F32)) for i in range(2)]
            gd.wait(s_pe, pe.ctr.v)
            bsd = gd.dma(out=bias[:], in_=i_bias)
            ve.waitd(bsd)

            pdep = [0, 0]
            atdep = [0, 0]
            kdep = [0, 0]
            bldep = [0, 0]
            s1dep = [0, 0]
            s2dep = [None, None]
            xcdep = 0
            kv_ = [None, None]
            bl_ = [None, None]
            bs_ = [None, None]
            kv_[0] = gd.dma(out=kn[0][:].rearrange("p (a o) -> p a o", a=32),
                            in_=fview(i_kern)[:, :, 0:CH])
            bl_[0] = gd.dma(out=ballb[0][:].rearrange("p (a o) -> p a o", a=4),
                            in_=i_ball.rearrange("(a p) o -> p a o", p=128)[:, :, 0:CH])
            bs_[0] = gd.dma(out=bsb[0][:], in_=i_bsT[:, 0:CH])
            for oc in range(8):
                ki = oc % 2
                if oc + 1 < 8:
                    ni = (oc + 1) % 2
                    gd.wait(s_pe, kdep[ni])
                    kv_[ni] = gd.dma(
                        out=kn[ni][:].rearrange("p (a o) -> p a o", a=32),
                        in_=fview(i_kern)[:, :, (oc + 1) * CH:(oc + 2) * CH])
                    gd.wait(s_pe, bldep[ni])
                    bl_[ni] = gd.dma(
                        out=ballb[ni][:].rearrange("p (a o) -> p a o", a=4),
                        in_=i_ball.rearrange("(a p) o -> p a o", p=128)[
                            :, :, (oc + 1) * CH:(oc + 2) * CH])
                    bs_[ni] = gd.dma(out=bsb[ni][:],
                                     in_=i_bsT[:, (oc + 1) * CH:(oc + 2) * CH])
                xcv = None
                for tt in range(8):
                    if tt % 4 == 0:
                        sy.wait(s_ve, xcdep)
                        xcv = sy.dma(out=xc[:].rearrange("p (a t) -> p a t", a=4),
                                     in_=i_xown.rearrange("(a p) d -> p a d", p=128)[
                                         :, tt // 4 * 4:tt // 4 * 4 + 4,
                                         oc * CH:(oc + 1) * CH])
                    par = tt % 2
                    pb = pf[par]
                    api = tt % 2
                    sy.wait(s_ve, atdep[api])
                    av_ = sy.dma(out=at[api][:],
                                 in_=sc_at[tt * 128:(tt + 1) * 128,
                                           oc * CH:(oc + 1) * CH])
                    pe.waitd(kv_[ki])
                    pe.waitd(bl_[ki])
                    pe.waitd(bs_[ki])
                    pe.wait(s_ve, pdep[par])
                    for dc in range(32):
                        pe.call("matmul", pb[:],
                                xq[:, dc * TOWN + (tt // 4) * CH + (tt % 4) * 128:
                                   dc * TOWN + (tt // 4) * CH + (tt % 4 + 1) * 128],
                                kn[ki][:, dc * CH:(dc + 1) * CH],
                                start=(dc == 0), stop=False, inc=False)
                    for nt in range(4):
                        pe.call("matmul", pb[:],
                                sb_axg[:, nt * TOWN + (tt // 4) * CH + (tt % 4) * 128:
                                       nt * TOWN + (tt // 4) * CH + (tt % 4 + 1) * 128],
                                ballb[ki][:, nt * CH:(nt + 1) * CH],
                                start=False, stop=False, inc=False)
                    pe.call("matmul", pb[:],
                            sb_axs[:, (tt // 4) * CH + (tt % 4) * 128:
                                   (tt // 4) * CH + (tt % 4 + 1) * 128],
                            bsb[ki][:, :],
                            start=False, stop=True)
                    pv = pe.ctr.v
                    kdep[ki] = pv
                    bldep[ki] = pv
                    ve.wait(s_pe, pv)
                    ve.waitd(av_)
                    ve.waitd(s2dep[par])
                    v1 = ve.call("tensor_add", st1[par][:], pb[:], at[api][:])
                    pdep[par] = v1
                    atdep[api] = v1
                    ve.waitd(xcv)
                    ve.wait(s_ve, v1)
                    v2 = ve.call("tensor_add", st1[par][:], st1[par][:],
                                 xc[:, (tt % 4) * CH:(tt % 4 + 1) * CH])
                    ve.wait(s_ve, v2)
                    v3 = ve.call("tensor_add", st1[par][:], st1[par][:],
                                 bias[:, oc * CH:(oc + 1) * CH])
                    xcdep = v3
                    s1dep[par] = v3
                    sy.wait(s_ve, v3)
                    sv = sy.dma(out=o_out[tt * 128:(tt + 1) * 128,
                                          oc * CH:(oc + 1) * CH],
                                in_=st1[par][:])
                    s2dep[par] = sv

    final_phase()
    for sem, v in sy.frontier():
        sy.wait(sem, v)

    with nc.Block() as block:

        @block.gpsimd
        def _(g):
            for f in gd.plan:
                f(g)

        @block.sync
        def _(s):
            for f in sy.plan:
                f(s)

        @block.tensor
        def _(t):
            for f in pe.plan:
                f(t)

        @block.vector
        def _(v):
            for f in ve.plan:
                f(v)

        @block.scalar
        def _(s):
            for f in ac.plan:
                f(s)

    stack.close()
    return nc


def _rope_tables():
    inv = 1.0 / (500000.0 ** (np.arange(0, HD, 2, dtype=np.float64) / HD))
    t = np.arange(S, dtype=np.float64)
    ang = np.outer(inv, t)
    cos = np.cos(ang).astype(np.float32)
    sin = np.sin(ang).astype(np.float32)
    cos2 = np.concatenate([cos, cos], 0)
    sin2 = np.concatenate([sin, sin], 0)
    return cos2, sin2


def _half_perm(nheads):
    perm = []
    for p in range(nheads // 2):
        for half in range(2):
            for h in (2 * p, 2 * p + 1):
                base = h * HD + half * 64
                perm.extend(range(base, base + 64))
    return np.asarray(perm)


_NC_CACHE = {}


def prepare_in_maps(L0_attn_x, L0_attn_w_qkv, L0_attn_w_o, L0_ffn_gate,
                    L0_ffn_kernel, L0_ffn_bias, L0_ffn_lora_A, L0_ffn_lora_B,
                    L0_ffn_lora_A_shared, L0_ffn_lora_B_shared):
    x = np.ascontiguousarray(np.asarray(L0_attn_x, dtype=np.float32))
    w_qkv = np.asarray(L0_attn_w_qkv, dtype=np.float32)
    w_o = np.asarray(L0_attn_w_o, dtype=np.float32)
    gate = np.asarray(L0_ffn_gate, dtype=np.float32)
    kern = np.asarray(L0_ffn_kernel, dtype=np.float32)
    biasv = np.asarray(L0_ffn_bias, dtype=np.float32)
    lA = np.asarray(L0_ffn_lora_A, dtype=np.float32)
    lB = np.asarray(L0_ffn_lora_B, dtype=np.float32)
    lAs = np.asarray(L0_ffn_lora_A_shared, dtype=np.float32)
    lBs = np.asarray(L0_ffn_lora_B_shared, dtype=np.float32)

    permq = _half_perm(NH)
    permk = _half_perm(KVH)
    wqT = np.ascontiguousarray(w_qkv[:D].T[:, permq])
    wkT = np.ascontiguousarray(w_qkv[D:D + KVH * HD].T[:, permk])
    wvT = np.ascontiguousarray(w_qkv[D + KVH * HD:].T)
    woT = np.ascontiguousarray(w_o.T)
    biasT = np.ascontiguousarray(np.broadcast_to(biasv, (128, D)))
    aallT = np.ascontiguousarray(lA.transpose(2, 0, 1).reshape(D, NRTOT))
    ball = np.ascontiguousarray(lB.transpose(0, 2, 1).reshape(NRTOT, D))
    asT = np.ascontiguousarray(lAs.T)
    bsT = np.ascontiguousarray(lBs.T)
    cos2, sin2 = _rope_tables()

    cm4 = np.zeros((128, 4 * CH), dtype=bfdt)
    for kt in range(4):
        kvloc = kt * 128 + np.arange(128)[:, None]
        cm4[:, kt * CH:(kt + 1) * CH] = (kvloc <= np.arange(CH)[None, :]).astype(bfdt)
    onesf = np.ones((1, 128), np.float32)
    onesb = np.ones((128, 1), bfdt)

    in_maps = []
    slices = []
    for c in range(8):
        b, j = c // 4, c % 4
        A_real, B_real = j, 7 - j
        place = [-1] * NCHUNK
        place[3] = A_real
        place[7] = B_real
        for i in range(j):
            place[i] = i
        rest = [cchunk for cchunk in range(j + 1, 7 - j)]
        free = [p for p in range(7) if place[p] < 0]
        for p, cchunk in zip(free, rest):
            place[p] = cchunk
        leftover = [cchunk for cchunk in range(NCHUNK) if cchunk not in place]
        li = 0
        for p in range(NCHUNK):
            if place[p] < 0:
                place[p] = leftover[li]
                li += 1
        kvtok = np.concatenate([np.arange(place[p] * CH, (place[p] + 1) * CH)
                                for p in range(NCHUNK)])
        owntok = np.concatenate([np.arange(A_real * CH, (A_real + 1) * CH),
                                 np.arange(B_real * CH, (B_real + 1) * CH)])
        xb = x[b].reshape(S, D)
        xT = xb.T
        xTkv = np.ascontiguousarray(xT[:, kvtok])
        xown = np.ascontiguousarray(xb[owntok])

        btab = np.zeros((128, 12), np.float32)
        for v, (qb, p) in enumerate(VISITS):
            if v in DIAG:
                btab[:, v] = -SHIFT
            else:
                qc = A_real if qb == 0 else B_real
                btab[:, v] = -SHIFT if place[p] < qc else EMPTY

        lbA = b * 4 + A_real // 2
        lbB = b * 4 + B_real // 2
        gnr = np.zeros((128, 8), np.float32)
        p_idx = np.arange(128)
        for tc, lb in ((0, lbA), (1, lbB)):
            for nt in range(4):
                gnr[:, tc * 4 + nt] = gate[lb, 4 * nt + p_idx // 32]

        in_maps.append({
            "xTkv": xTkv, "xown": xown,
            "wqT": wqT, "wkT": wkT, "wvT": wvT, "woT": woT,
            "kern": kern, "biasT": biasT,
            "aallT": aallT, "ball": ball, "asT": asT, "bsT": bsT,
            "gatenr": gnr,
            "cos_kv": np.ascontiguousarray(cos2[:, kvtok]),
            "sin_kv": np.ascontiguousarray(sin2[:, kvtok]),
            "cos_q": np.ascontiguousarray(cos2[:, owntok]),
            "sin_q": np.ascontiguousarray(sin2[:, owntok]),
            "cmask4": cm4, "btab": btab,
            "onesf": onesf, "onesb": onesb, "onesbr": np.ones((1, 128), bfdt),
            "epsc": np.full((128, 1), EPS, np.float32),
        })
        slices.append((b, owntok))
    return in_maps, slices


def kernel(**inputs):
    in_maps, slices = prepare_in_maps(**inputs)
    if "nc" not in _NC_CACHE:
        _NC_CACHE["nc"] = build_program()
    nc = _NC_CACHE["nc"]
    res = run_bass_kernel_spmd(nc, in_maps, core_ids=list(range(8)))
    out = np.empty((2, S, D), dtype=np.float32)
    for c in range(8):
        b, owntok = slices[c]
        out[b, owntok, :] = res.results[c]["out"]
    return out.reshape(2, S, D)




# version 16
# speedup vs baseline: 1.1676x; 1.1676x over previous
import numpy as np
import ml_dtypes
import concourse.bass as bass
import concourse.mybir as mybir
from concourse.bass_utils import run_bass_kernel_spmd

F32 = mybir.dt.float32
BF16 = mybir.dt.bfloat16
AF = mybir.ActivationFunctionType
ALU = mybir.AluOpType
bfdt = ml_dtypes.bfloat16

D = 4096
S = 4096
NH = 32
KVH = 8
HD = 128
CH = 512
NCHUNK = 8
TOWN = 1024
RANK = 32
NLORA = 16
NRTOT = NLORA * RANK
SHIFT = 12.0
EMPTY = -30000.0
SCALE = float(1.0 / np.sqrt(128.0))
EPS = 1e-5

VISITS = [(0, p) for p in range(4)] + [(1, p) for p in range(8)]
DIAG = {3, 11}


class Ctr:
    def __init__(self, step=1):
        self.v = 0
        self.step = step

    def tick(self):
        self.v += self.step
        return self.v


class Rec:

    def __init__(self, sem, ctr):
        self.sem = sem
        self.ctr = ctr
        self.plan = []
        self.marks = {}

    def wait(self, sem, val):
        if val is None or val <= 0:
            return
        k = id(sem)
        if self.marks.get(k, 0) >= val:
            return
        self.marks[k] = val

        def f(eng, s=sem, v=val):
            eng.wait_ge(s, v)

        self.plan.append(f)

    def waitd(self, dep):
        if dep is None:
            return
        if isinstance(dep, list):
            for d in dep:
                self.waitd(d)
            return
        self.wait(dep[0], dep[1])

    def frontier(self):
        return [(self.sem, self.ctr.v)]

    def call(self, meth, *a, inc=True, **kw):
        if inc:
            v = self.ctr.tick()

            def f(eng, m=meth, aa=a, kk=kw, s=self.sem, st=self.ctr.step):
                getattr(eng, m)(*aa, **kk).then_inc(s, st)
        else:
            v = self.ctr.v

            def f(eng, m=meth, aa=a, kk=kw):
                getattr(eng, m)(*aa, **kk)

        self.plan.append(f)
        return v


class DmaRec(Rec):

    def __init__(self, sems):
        super().__init__(sems[0], Ctr(16))
        self.sems = sems
        self.ctrs = [Ctr(16) for _ in sems]
        self.rr = 0

    def frontier(self):
        return [(s, c.v) for s, c in zip(self.sems, self.ctrs)]

    def dma(self, **kw):
        i = self.rr % len(self.sems)
        self.rr += 1
        sem, ctr = self.sems[i], self.ctrs[i]
        self.wait(sem, ctr.v)
        v = ctr.tick()

        def f(eng, kk=kw, s=sem, vv=v):
            eng.dma_start(**kk).then_inc(s, 16)

        self.plan.append(f)
        return (sem, v)


def build_program():
    nc = bass.Bass("TRN2", target_bir_lowering=False, debug=False)
    dram = nc.dram_tensor

    i_xTkv = dram("xTkv", [D, S], F32, kind="ExternalInput").ap()
    i_xown = dram("xown", [TOWN, D], F32, kind="ExternalInput").ap()
    i_wqT = dram("wqT", [D, D], F32, kind="ExternalInput").ap()
    i_wkT = dram("wkT", [D, KVH * HD], F32, kind="ExternalInput").ap()
    i_wvT = dram("wvT", [D, KVH * HD], F32, kind="ExternalInput").ap()
    i_woT = dram("woT", [D, D], F32, kind="ExternalInput").ap()
    i_kern = dram("kern", [D, D], F32, kind="ExternalInput").ap()
    i_bias = dram("biasT", [128, D], F32, kind="ExternalInput").ap()
    i_aall = dram("aallT", [D, NRTOT], F32, kind="ExternalInput").ap()
    i_ball = dram("ball", [NRTOT, D], F32, kind="ExternalInput").ap()
    i_asT = dram("asT", [D, RANK], F32, kind="ExternalInput").ap()
    i_bsT = dram("bsT", [RANK, D], F32, kind="ExternalInput").ap()
    i_gnr = dram("gatenr", [128, 8], F32, kind="ExternalInput").ap()
    i_cos_kv = dram("cos_kv", [128, S], F32, kind="ExternalInput").ap()
    i_sin_kv = dram("sin_kv", [128, S], F32, kind="ExternalInput").ap()
    i_cos_q = dram("cos_q", [128, TOWN], F32, kind="ExternalInput").ap()
    i_sin_q = dram("sin_q", [128, TOWN], F32, kind="ExternalInput").ap()
    i_cm4 = dram("cmask4", [128, 4 * CH], BF16, kind="ExternalInput").ap()
    i_btab = dram("btab", [128, 12], F32, kind="ExternalInput").ap()
    i_onesf = dram("onesf", [1, 128], F32, kind="ExternalInput").ap()
    i_onesb = dram("onesb", [128, 1], BF16, kind="ExternalInput").ap()
    i_onesbr = dram("onesbr", [1, 128], BF16, kind="ExternalInput").ap()
    i_epsc = dram("epsc", [128, 1], F32, kind="ExternalInput").ap()
    o_out = dram("out", [TOWN, D], F32, kind="ExternalOutput").ap()

    sc_xkv = dram("sc_xkv", [D, S], BF16, kind="Internal").ap()
    sc_q = dram("sc_q", [NH, 128, TOWN], BF16, kind="Internal").ap()
    sc_k = dram("sc_k", [KVH // 2, NCHUNK, 2, 128, CH], BF16, kind="Internal").ap()
    sc_v = dram("sc_v", [KVH, NCHUNK, CH, HD], BF16, kind="Internal").ap()
    sc_at = dram("sc_at", [TOWN, D], BF16, kind="Internal").ap()

    from contextlib import ExitStack
    stack = ExitStack()
    e = stack.enter_context

    NQ = 5
    gd_sems = [e(nc.semaphore(f"s_gd{i}")) for i in range(NQ)]
    sy_sems = [e(nc.semaphore(f"s_sy{i}")) for i in range(NQ)]
    s_pe = e(nc.semaphore("s_pe"))
    s_ve = e(nc.semaphore("s_ve"))
    s_ac = e(nc.semaphore("s_ac"))
    s_bar = e(nc.semaphore("s_bar"))

    gd = DmaRec(gd_sems)
    sy = DmaRec(sy_sems)
    pe = Rec(s_pe, Ctr())
    ve = Rec(s_ve, Ctr())
    ac = Rec(s_ac, Ctr())
    bar_n = [0]

    def barrier(gd_front=None, sy_front=None, skip_gd=False):
        for r in (pe, ve, ac):
            for sem, v in r.frontier():
                sy.wait(sem, v)
        for sem, v in (gd_front if gd_front is not None else gd.frontier()):
            sy.wait(sem, v)
        for sem, v in (sy_front if sy_front is not None else sy.frontier()):
            sy.wait(sem, v)
        bar_n[0] += 1

        def f(eng):
            eng.sem_inc(s_bar, 1)

        sy.plan.append(f)
        sy.marks[id(s_bar)] = bar_n[0]
        for r in ((pe, ve, ac) if skip_gd else (pe, ve, ac, gd)):
            r.wait(s_bar, bar_n[0])

    sb_onesf = e(nc.sbuf_tensor("p_onesf", [1, 128], F32))
    sb_onesb = e(nc.sbuf_tensor("p_onesb", [128, 1], BF16))
    sb_onesbr = e(nc.sbuf_tensor("p_onesbr", [1, 128], BF16))
    sb_btab = e(nc.sbuf_tensor("p_btab", [128, 12], F32))
    sb_gnr = e(nc.sbuf_tensor("p_gnr", [128, 8], F32))
    sb_cpool = e(nc.sbuf_tensor("p_cpool", [128, 1], F32))
    sb_axg = e(nc.sbuf_tensor("p_axg", [128, 4 * TOWN], BF16))
    sb_axs = e(nc.sbuf_tensor("p_axs", [RANK, TOWN], BF16))
    nc.const_aps.register(F32, EPS, sb_cpool[:, 0:1])

    g_base = []
    for dst, src in ((sb_onesf, i_onesf), (sb_onesb, i_onesb), (sb_onesbr, i_onesbr),
                     (sb_btab, i_btab), (sb_gnr, i_gnr), (sb_cpool, i_epsc)):
        g_base.append(gd.dma(out=dst[:], in_=src))

    def fview(t):
        return t.rearrange("(a p) t -> p a t", p=128)

    def ln_pass(xT_in, ntok, sc_out, tag, hook=None, post_hook=None):
        nch = ntok // CH
        SLAB = 4 * CH
        with ExitStack() as ps:
            pp = ps.enter_context
            xch = [pp(nc.sbuf_tensor(f"l{tag}_x{i}", [128, 32 * CH], BF16)) for i in range(3)]
            sq = [pp(nc.sbuf_tensor(f"l{tag}_q{i}", [128, CH], BF16)) for i in range(2)]
            tb = [pp(nc.sbuf_tensor(f"l{tag}_t{i}", [128, CH], BF16)) for i in range(2)]
            srow = [pp(nc.sbuf_tensor(f"l{tag}_s{i}", [1, CH], F32)) for i in range(2)]
            qrow = [pp(nc.sbuf_tensor(f"l{tag}_ss{i}", [1, CH], F32)) for i in range(2)]
            rrow = pp(nc.sbuf_tensor(f"l{tag}_r", [1, CH], F32))
            nrow = [pp(nc.sbuf_tensor(f"l{tag}_n{i}", [1, CH], BF16)) for i in range(2)]
            rbrow = [pp(nc.sbuf_tensor(f"l{tag}_rb{i}", [1, CH], BF16)) for i in range(2)]
            bcm = pp(nc.sbuf_tensor(f"l{tag}_bm", [128, CH], BF16))
            bcr = pp(nc.sbuf_tensor(f"l{tag}_br", [128, CH], BF16))
            p_sum = pp(nc.psum_tensor(f"l{tag}_ps", [1, CH], F32))
            p_sq = pp(nc.psum_tensor(f"l{tag}_pq", [1, CH], F32))
            p_bc = pp(nc.psum_tensor(f"l{tag}_pb", [128, 2 * CH], F32))
            xload = [None, None, None]
            xfree = [0, 0, 0]
            xttl = [None, None, None]
            xpe = [0, 0, 0]
            xstore = [None, None, None]
            qdep = [0, 0]
            tdep = [0, 0]
            row_done = 0
            rowfin = [0, 0]
            rowpe = [0, 0]
            rowac = [0]
            stats_done = {}

            def do_stats(tc):
                nonlocal row_done
                xi = tc % 3
                ri = tc % 2
                pe.wait(s_ve, row_done)
                for dc in range(32):
                    si = dc % 2
                    pe.waitd(xload[xi][dc // 8])
                    ac.waitd(xload[xi][dc // 8])
                    ac.wait(s_pe, qdep[si])
                    av = ac.call("activation", sq[si][:],
                                 xch[xi][:, dc * CH:(dc + 1) * CH], AF.Square)
                    pe.wait(s_ac, av)
                    pe.call("matmul", p_sum[0:1, :], sb_onesb[:, 0:1],
                            xch[xi][:, dc * CH:(dc + 1) * CH],
                            start=(dc == 0), stop=(dc == 31))
                    pq = pe.call("matmul", p_sq[0:1, :], sb_onesb[:, 0:1],
                                 sq[si][:],
                                 start=(dc == 0), stop=(dc == 31))
                    qdep[si] = pq
                stats_done[tc] = pe.ctr.v
                ve.wait(s_pe, stats_done[tc])
                ve.wait(s_ve, ve.ctr.v)
                ve.call("tensor_copy", srow[ri][0:1, :], p_sum[0:1, :])
                row_done = ve.call("tensor_copy", qrow[ri][0:1, :], p_sq[0:1, :])

            def do_rows(tc):
                ri = tc % 2
                ve.wait(s_ve, ve.ctr.v)
                ve.wait(s_ac, rowac[0])
                m1 = ve.call("tensor_scalar", srow[ri][0:1, :], srow[ri][0:1, :],
                             1.0 / D, None, ALU.mult)
                ve.wait(s_ve, m1)
                v1 = ve.call("tensor_mul", rrow[0:1, :], srow[ri][0:1, :],
                             srow[ri][0:1, :])
                ve.wait(s_ve, v1)
                v2 = ve.call("scalar_tensor_tensor", rrow[0:1, :], qrow[ri][0:1, :],
                             1.0 / D, rrow[0:1, :], ALU.mult, ALU.subtract)
                ac.wait(s_ve, v2)
                ac.waitd(g_base)
                a1 = ac.call("activation", rrow[0:1, :], rrow[0:1, :], AF.Ln, bias=EPS)
                ac.wait(s_ac, a1)
                a2 = ac.call("activation", rrow[0:1, :], rrow[0:1, :], AF.Exp,
                             scale=-0.5)
                rowac[0] = a2
                ve.wait(s_ac, a2)
                ve.wait(s_pe, rowpe[ri])
                ve.call("tensor_scalar", nrow[ri][0:1, :], srow[ri][0:1, :], -1.0,
                        None, ALU.mult)
                n2 = ve.call("tensor_copy", rbrow[ri][0:1, :], rrow[0:1, :])
                rowfin[ri] = n2

            def do_tts(tc):
                xi = tc % 3
                ri = tc % 2
                pe.wait(s_ve, rowfin[ri])
                pe.call("matmul", p_bc[:, 0:CH], sb_onesbr[0:1, :],
                        nrow[ri][0:1, :], start=True, stop=True)
                pb = pe.call("matmul", p_bc[:, CH:2 * CH], sb_onesbr[0:1, :],
                             rbrow[ri][0:1, :], start=True, stop=True)
                rowpe[ri] = pb
                ve.wait(s_pe, pb)
                ve.call("tensor_copy", bcm[:], p_bc[:, 0:CH])
                bcv = ve.call("tensor_copy", bcr[:], p_bc[:, CH:2 * CH])
                ve.wait(s_ve, bcv)
                last_tt = 0
                stores = []
                ttl = []
                for dc in range(32):
                    ti = dc % 2
                    ve.wait(s_ve, tdep[ti])
                    xs = xch[xi][:, dc * CH:(dc + 1) * CH]
                    v1 = ve.call("tensor_add", tb[ti][:], xs, bcm[:])
                    ve.wait(s_ve, v1)
                    v2 = ve.call("tensor_mul", xs, tb[ti][:], bcr[:])
                    tdep[ti] = v2
                    last_tt = v2
                    ttl.append(v2)
                    if dc % 2 == 1:
                        sy.wait(s_ve, v2)
                        stores.append(sy.dma(
                            out=fview(sc_out)[:, dc - 1:dc + 1,
                                              tc * CH:(tc + 1) * CH],
                            in_=xch[xi][:, (dc - 1) * CH:(dc + 1) * CH].rearrange(
                                "p (a t) -> p a t", a=2)))
                xfree[xi] = last_tt
                xttl[xi] = ttl
                xstore[xi] = stores

            def load_chunk(tc):
                xi = tc % 3
                gd.wait(s_ve, xfree[xi])
                gd.wait(s_pe, xpe[xi])
                gd.waitd(xstore[xi])
                deps = []
                for pc in range(4):
                    deps.append(gd.dma(
                        out=xch[xi][:].rearrange("p (a t) -> p a t", a=32)[
                            :, pc * 8:(pc + 1) * 8, :],
                        in_=fview(xT_in)[:, pc * 8:(pc + 1) * 8,
                                         tc * CH:(tc + 1) * CH]))
                xload[xi] = deps

            load_chunk(0)
            for tc in range(nch):
                if hook is not None:
                    hook(tc)
                do_stats(tc)
                do_rows(tc)
                if tc > 0:
                    do_tts(tc - 1)
                    if post_hook is not None:
                        xpe[(tc - 1) % 3] = post_hook(tc - 1, xch[(tc - 1) % 3],
                                                      xttl[(tc - 1) % 3])
                if tc + 1 < nch:
                    load_chunk(tc + 1)
            do_tts(nch - 1)
            if post_hook is not None:
                xpe[(nch - 1) % 3] = post_hook(nch - 1, xch[(nch - 1) % 3],
                                               xttl[(nch - 1) % 3])

    def rope_block(b0, b1, cs, sn, bufs, dst0, dst1):
        er, orr, t1, t2, ke, ko = bufs
        ve.call("tensor_copy", er[:], b0[:])
        c2 = ve.call("tensor_copy", orr[:], b1[:])
        ve.wait(s_ve, c2)
        ve.call("tensor_mul", t1[:], er[:], cs)
        v2 = ve.call("scalar_tensor_tensor", t2[:], orr[:], -1.0, sn, ALU.mult, ALU.mult)
        ve.wait(s_ve, v2)
        v3 = ve.call("tensor_add", ke[:], t1[:], t2[:])
        ve.wait(s_ve, v3)
        ve.call("tensor_mul", t1[:], orr[:], cs)
        v5 = ve.call("tensor_mul", t2[:], er[:], sn)
        ve.wait(s_ve, v5)
        v6 = ve.call("tensor_add", ko[:], t1[:], t2[:])
        sy.wait(s_ve, v3)
        s1 = sy.dma(out=dst0, in_=ke[:])
        sy.wait(s_ve, v6)
        s2 = sy.dma(out=dst1, in_=ko[:])
        return c2, v6, [s1, s2]

    def kv_sweeps(wsb, kscr, pk, xp, xqh0, xqh1, wv_deps, xv0):
        with ExitStack() as ps:
            pv4 = [ps.enter_context(nc.psum_tensor(f"v_p{i}", [128, CH], F32))
                   for i in range(4)]
            bufs = [xp[0], xqh0, xqh1]
            idxmap = [0, 2, 0, 1, 0, 2, 0, 2]
            xdep = [0, 0, 0]
            xv = [None] * NCHUNK
            xv[0] = xv0
            sdep = [None, None]
            vstage = [kscr[4], kscr[5]]
            pdep = [0, 0]

            def loadx(tc):
                bi = idxmap[tc]
                sy.wait(s_pe, xdep[bi])
                xv[tc] = sy.dma(out=bufs[bi][:],
                                in_=fview(sc_xkv)[:, :, tc * CH:(tc + 1) * CH])

            loadx(1)
            pall = [pk[0], pk[1], pk[2], pk[3], pv4[0], pv4[1], pv4[2], pv4[3]]
            pe.waitd(xv[0])
            for dc in range(32):
                if dc % 4 == 0:
                    pe.waitd(wv_deps[dc // 4])
                for b in range(8):
                    tt, vh = b // 2, b % 2
                    pe.call("matmul", pall[b][:],
                            bufs[0][:, dc * CH + tt * 128:dc * CH + (tt + 1) * 128],
                            wsb[:, dc * 1024 + vh * CH:dc * 1024 + (vh + 1) * CH],
                            start=(dc == 0), stop=(dc == 31),
                            inc=(dc == 31 and b == 7))
            pv = pe.ctr.v
            xdep[0] = pv
            for b in range(8):
                tt, vh = b // 2, b % 2
                par = b % 2
                ve.wait(s_pe, pv)
                ve.waitd(sdep[par])
                cv = ve.call("tensor_copy", vstage[par][:], pall[b][:])
                if b < 2:
                    pdep[b] = cv
                sy.wait(s_ve, cv)
                svl = []
                for hh in range(4):
                    h = vh * 4 + hh
                    svl.append(sy.dma(
                        out=sc_v[h, 0, tt * 128:(tt + 1) * 128, :],
                        in_=vstage[par][:, hh * 128:(hh + 1) * 128]))
                sdep[par] = svl
            for tc in range(1, NCHUNK):
                bi = idxmap[tc]
                if tc + 1 < NCHUNK:
                    loadx(tc + 1)
                pe.waitd(xv[tc])
                for tt in range(4):
                    for vh in range(2):
                        par = vh % 2
                        pb = pk[par]
                        pe.wait(s_ve, pdep[par])
                        for dc in range(32):
                            pe.call("matmul", pb[:],
                                    bufs[bi][:, dc * CH + tt * 128:
                                             dc * CH + (tt + 1) * 128],
                                    wsb[:, dc * 1024 + vh * CH:
                                        dc * 1024 + (vh + 1) * CH],
                                    start=(dc == 0), stop=(dc == 31), inc=(dc == 31))
                        pv = pe.ctr.v
                        xdep[bi] = pv
                        ve.wait(s_pe, pv)
                        ve.waitd(sdep[par])
                        cv = ve.call("tensor_copy", vstage[par][:], pb[:])
                        pdep[par] = cv
                        sy.wait(s_ve, cv)
                        svl = []
                        for hh in range(4):
                            h = vh * 4 + hh
                            svl.append(sy.dma(
                                out=sc_v[h, tc, tt * 128:(tt + 1) * 128, :],
                                in_=vstage[par][:, hh * 128:(hh + 1) * 128]))
                        sdep[par] = svl



    def q_proj(xqh0, xqh1, pk, kcs, ksn, kscr, aall, asb, lg):
        with ExitStack() as ps:
            pp = ps.enter_context
            wq = [pp(nc.sbuf_tensor(f"q_w{i}", [128, 32 * 128], BF16))
                  for i in range(2)]
            xqh = [xqh0, xqh1]
            cd = [gd.dma(out=kcs[t][:], in_=i_cos_q[:, t * CH:(t + 1) * CH])
                  for t in range(2)]
            sd = [gd.dma(out=ksn[t][:], in_=i_sin_q[:, t * CH:(t + 1) * CH])
                  for t in range(2)]
            ve.waitd(cd)
            ve.waitd(sd)
            wdep = [0, 0]
            pdep = [0, 0, 0, 0]
            sdep = None
            rdep = 0
            gv = [None, None]
            gv[0] = gd.dma(out=wq[0][:].rearrange("p (a o) -> p a o", a=32),
                           in_=fview(i_wqT)[:, :, 0:128])
            for pr in range(NH // 2):
                pvt = [0, 0]
                for idx in range(2):
                    hp = 2 * pr + idx
                    wi = hp % 2
                    if hp + 1 < NH:
                        ni = (hp + 1) % 2
                        gd.wait(s_pe, wdep[ni])
                        gv[ni] = gd.dma(
                            out=wq[ni][:].rearrange("p (a o) -> p a o", a=32),
                            in_=fview(i_wqT)[:, :, (hp + 1) * 128:(hp + 2) * 128])
                    if hp == 4:
                        lg.append(gd.dma(out=aall[:].rearrange("p (a r) -> p a r", a=32),
                                         in_=fview(i_aall)))
                        lg.append(gd.dma(out=asb[:].rearrange("p (a r) -> p a r", a=32),
                                         in_=fview(i_asT)))
                    pe.waitd(gv[wi])
                    for tc in range(2):
                        pb = pk[2 * tc + idx]
                        pe.wait(s_ve, pdep[2 * tc + idx])
                        for dc in range(32):
                            pe.call("matmul", pb[:],
                                    wq[wi][:, dc * 128:(dc + 1) * 128],
                                    xqh[tc][:, dc * CH:(dc + 1) * CH],
                                    start=(dc == 0), stop=(dc == 31), inc=(dc == 31))
                        pvt[tc] = pe.ctr.v
                    wdep[wi] = pe.ctr.v
                for tc in range(2):
                    b0, b1 = pk[2 * tc], pk[2 * tc + 1]
                    ve.wait(s_pe, pvt[tc])
                    ve.wait(s_ve, rdep)
                    ve.waitd(sdep)
                    c2, v6, s2 = rope_block(
                        b0, b1, kcs[tc][:], ksn[tc][:], kscr[0:6],
                        sc_q[2 * pr, :, tc * CH:(tc + 1) * CH],
                        sc_q[2 * pr + 1, :, tc * CH:(tc + 1) * CH])
                    pdep[2 * tc] = c2
                    pdep[2 * tc + 1] = c2
                    rdep = v6
                    sdep = s2

    def lora_pre(xqh0, xqh1, aall, asb, lg):
        with ExitStack() as ps:
            pp = ps.enter_context
            pl = pp(nc.psum_tensor("lp_pl", [128, CH], F32))
            pls = pp(nc.psum_tensor("lp_pls", [RANK, CH], F32))
            xqh = [xqh0, xqh1]
            pe.waitd(lg)
            for nt in range(4):
                for tc in range(2):
                    for dc in range(32):
                        pe.call("matmul", pl[:],
                                aall[:, dc * NRTOT + nt * 128:dc * NRTOT + (nt + 1) * 128],
                                xqh[tc][:, dc * CH:(dc + 1) * CH],
                                start=(dc == 0), stop=(dc == 31), inc=(dc == 31))
                    pv = pe.ctr.v
                    ve.wait(s_pe, pv)
                    vv = ve.call("tensor_scalar",
                                 sb_axg[:, nt * TOWN + tc * CH:nt * TOWN + (tc + 1) * CH],
                                 pl[:], sb_gnr[:, tc * 4 + nt:tc * 4 + nt + 1], None,
                                 ALU.mult)
                    pe.wait(s_ve, vv)
            for tc in range(2):
                for dc in range(32):
                    pe.call("matmul", pls[:],
                            asb[:, dc * RANK:(dc + 1) * RANK],
                            xqh[tc][:, dc * CH:(dc + 1) * CH],
                            start=(dc == 0), stop=(dc == 31), inc=(dc == 31))
                pv = pe.ctr.v
                ve.wait(s_pe, pv)
                vv = ve.call("tensor_copy", sb_axs[:, tc * CH:(tc + 1) * CH], pls[:])
                pe.wait(s_ve, vv)

    with ExitStack() as ps_lnkv:
        plk = ps_lnkv.enter_context
        wsb = plk(nc.sbuf_tensor("kv_w", [128, 32 * KVH * HD], BF16))
        kcs = [plk(nc.sbuf_tensor(f"kc{i}", [128, CH], BF16)) for i in range(2)]
        ksn = [plk(nc.sbuf_tensor(f"ks{i}", [128, CH], BF16)) for i in range(2)]
        kscr = [plk(nc.sbuf_tensor(f"kscr{i}", [128, CH], BF16)) for i in range(10)]
        pk = [plk(nc.psum_tensor(f"kp{i}", [128, CH], F32)) for i in range(4)]
        wk_deps = []
        cs_deps = [None] * NCHUNK
        kstate = {"pdep": [0, 0, 0, 0], "sdep": [None, None], "rdep": 0,
                  "edep": 0, "csfree": [0, 0], "wkfree": [0] * 8}

        def wk_hook(tc):
            if tc < 2:
                for piece in range(4 * tc, 4 * tc + 4):
                    wk_deps.append(gd.dma(
                        out=wsb[:].rearrange("p (a o) -> p a o", a=32)[
                            :, piece * 4:(piece + 1) * 4, :],
                        in_=fview(i_wkT)[:, piece * 4:(piece + 1) * 4, :]))
            if tc < NCHUNK:
                ci = tc % 2
                gd.wait(s_ve, kstate["csfree"][ci])
                cs_deps[tc] = [
                    gd.dma(out=kcs[ci][:], in_=i_cos_kv[:, tc * CH:(tc + 1) * CH]),
                    gd.dma(out=ksn[ci][:], in_=i_sin_kv[:, tc * CH:(tc + 1) * CH])]

        def rope_pair(pv, half, tc, ci):
            er0, orr0, er1, orr1, t1, t2, ke0, ko0, ke1, ko1 = kscr
            ve.wait(s_pe, pv)
            ve.wait(s_ve, kstate["edep"])
            ve.call("tensor_copy", er0[:], pk[0][:])
            ve.call("tensor_copy", orr0[:], pk[1][:])
            ve.call("tensor_copy", er1[:], pk[2][:])
            c2 = ve.call("tensor_copy", orr1[:], pk[3][:])
            for g in range(4):
                kstate["pdep"][g] = c2
            ve.wait(s_ve, c2)
            cs, sn = kcs[ci][:], ksn[ci][:]
            vlast = 0
            for g2, (er, orr, ke, ko) in enumerate(
                    ((er0, orr0, ke0, ko0), (er1, orr1, ke1, ko1))):
                pr = 2 * half + g2
                ve.wait(s_ve, vlast)
                ve.call("tensor_mul", t1[:], er[:], cs)
                v2 = ve.call("scalar_tensor_tensor", t2[:], orr[:], -1.0, sn,
                             ALU.mult, ALU.mult)
                ve.wait(s_ve, v2)
                ve.waitd(kstate["sdep"][g2])
                v3 = ve.call("tensor_add", ke[:], t1[:], t2[:])
                ve.wait(s_ve, v3)
                ve.call("tensor_mul", t1[:], orr[:], cs)
                v5 = ve.call("tensor_mul", t2[:], er[:], sn)
                ve.wait(s_ve, v5)
                v6 = ve.call("tensor_add", ko[:], t1[:], t2[:])
                sy.wait(s_ve, v3)
                s1 = sy.dma(out=sc_k[pr, tc, 0], in_=ke[:])
                sy.wait(s_ve, v6)
                s2 = sy.dma(out=sc_k[pr, tc, 1], in_=ko[:])
                kstate["sdep"][g2] = [s1, s2]
                vlast = v6
            kstate["rdep"] = vlast
            kstate["edep"] = vlast

        def k_hook(tc, xch_t, ttlist):
            ci = tc % 2
            pe.waitd(wk_deps)
            ve.waitd(cs_deps[tc])
            for half in range(2):
                for dc in range(32):
                    if half == 0:
                        pe.wait(s_ve, ttlist[dc])
                    for g in range(4):
                        if dc == 0:
                            pe.wait(s_ve, kstate["pdep"][g])
                        idx = 4 * half + g
                        lastwk = (tc == NCHUNK - 1 and half == 1 and g == 3
                                  and dc % 4 == 3)
                        pe.call("matmul", pk[g][:],
                                wsb[:, dc * 1024 + idx * 128:
                                    dc * 1024 + (idx + 1) * 128],
                                xch_t[:, dc * CH:(dc + 1) * CH],
                                start=(dc == 0), stop=(dc == 31),
                                inc=(dc == 31 and g == 3) or lastwk)
                        if lastwk:
                            kstate["wkfree"][dc // 4] = pe.ctr.v
                pv = pe.ctr.v
                rope_pair(pv, half, tc, ci)
            kstate["csfree"][ci] = kstate["rdep"]
            return pe.ctr.v

        ln_pass(i_xTkv, S, sc_xkv, "kv", hook=wk_hook, post_hook=k_hook)
        xqh0 = plk(nc.sbuf_tensor("q_x0", [128, 32 * CH], BF16))
        xqh1 = plk(nc.sbuf_tensor("q_x1", [128, 32 * CH], BF16))
        with ExitStack() as psv:
            xp = [psv.enter_context(nc.sbuf_tensor("kv_x0", [128, 32 * CH], BF16))]
            gdf = gd.frontier()
            syf = sy.frontier()
            wv_deps = []
            for piece in range(8):
                gd.wait(s_pe, kstate["wkfree"][piece])
                wv_deps.append(gd.dma(
                    out=wsb[:].rearrange("p (a o) -> p a o", a=32)[
                        :, piece * 4:(piece + 1) * 4, :],
                    in_=fview(i_wvT)[:, piece * 4:(piece + 1) * 4, :]))
            xv0 = sy.dma(out=xp[0][:], in_=fview(sc_xkv)[:, :, 0:CH])
            barrier(gd_front=gdf, sy_front=syf)
            kv_sweeps(wsb, kscr, pk, xp, xqh0, xqh1, wv_deps, xv0)
        aall = plk(nc.sbuf_tensor("lp_aall", [128, 32 * NRTOT], BF16))
        asb = plk(nc.sbuf_tensor("lp_as", [128, 32 * RANK], BF16))
        lg = []
        barrier()
        q_proj(xqh0, xqh1, pk, kcs, ksn, kscr, aall, asb, lg)
        barrier()
        lora_pre(xqh0, xqh1, aall, asb, lg)
    syf_q = sy.frontier()
    barrier()


    def final_phase(kn0, bl0, bs0, fdeps):
        with ExitStack() as ps:
            pp = ps.enter_context
            xq = pp(nc.sbuf_tensor("f_x", [128, 32 * TOWN], BF16))
            ballb = [bl0, pp(nc.sbuf_tensor("f_bl1", [128, 4 * CH], BF16))]
            bsb = [bs0, pp(nc.sbuf_tensor("f_bs1", [RANK, CH], BF16))]
            pf = [pp(nc.psum_tensor(f"f_p{i}", [128, CH], F32)) for i in range(2)]
            pl = pp(nc.psum_tensor("f_pl", [128, CH], F32))
            pls = pp(nc.psum_tensor("f_pls", [RANK, CH], F32))

            xv = [[sy.dma(out=xq[:].rearrange("p (a t) -> p a t", a=32)[:, :, 0:CH],
                          in_=fview(sc_xkv)[:, :, 3 * CH:4 * CH])],
                  [sy.dma(out=xq[:].rearrange("p (a t) -> p a t", a=32)[:, :, CH:2 * CH],
                          in_=fview(sc_xkv)[:, :, 7 * CH:8 * CH])]]
            bias = pp(nc.sbuf_tensor("f_bias", [128, D], F32))
            kn = [kn0, pp(nc.sbuf_tensor("f_k1", [128, 32 * CH], BF16))]
            xc = pp(nc.sbuf_tensor("f_xc", [128, 4 * CH], F32))
            at = [pp(nc.sbuf_tensor(f"f_at{i}", [128, CH], BF16)) for i in range(2)]
            st1 = [pp(nc.sbuf_tensor(f"f_s1_{i}", [128, CH], F32)) for i in range(2)]
            bsd = gd.dma(out=bias[:], in_=i_bias)
            ve.waitd(bsd)

            pdep = [0, 0]
            atdep = [0, 0]
            kdep = [0, 0]
            bldep = [0, 0]
            s1dep = [0, 0]
            s2dep = [None, None]
            xcdep = 0
            kv_ = [fdeps, None]
            bl_ = [fdeps, None]
            bs_ = [fdeps, None]
            for oc in range(8):
                ki = oc % 2
                if oc + 1 < 8:
                    ni = (oc + 1) % 2
                    gd.wait(s_pe, kdep[ni])
                    kv_[ni] = gd.dma(
                        out=kn[ni][:].rearrange("p (a o) -> p a o", a=32),
                        in_=fview(i_kern)[:, :, (oc + 1) * CH:(oc + 2) * CH])
                    gd.wait(s_pe, bldep[ni])
                    bl_[ni] = gd.dma(
                        out=ballb[ni][:].rearrange("p (a o) -> p a o", a=4),
                        in_=i_ball.rearrange("(a p) o -> p a o", p=128)[
                            :, :, (oc + 1) * CH:(oc + 2) * CH])
                    bs_[ni] = gd.dma(out=bsb[ni][:],
                                     in_=i_bsT[:, (oc + 1) * CH:(oc + 2) * CH])
                xcv = None
                for tt in range(8):
                    if tt % 4 == 0:
                        sy.wait(s_ve, xcdep)
                        xcv = sy.dma(out=xc[:].rearrange("p (a t) -> p a t", a=4),
                                     in_=i_xown.rearrange("(a p) d -> p a d", p=128)[
                                         :, tt // 4 * 4:tt // 4 * 4 + 4,
                                         oc * CH:(oc + 1) * CH])
                    par = tt % 2
                    pb = pf[par]
                    api = tt % 2
                    sy.wait(s_ve, atdep[api])
                    av_ = sy.dma(out=at[api][:],
                                 in_=sc_at[tt * 128:(tt + 1) * 128,
                                           oc * CH:(oc + 1) * CH])
                    pe.waitd(kv_[ki])
                    pe.waitd(bl_[ki])
                    pe.waitd(bs_[ki])
                    pe.waitd(xv[tt // 4])
                    pe.wait(s_ve, pdep[par])
                    for dc in range(32):
                        pe.call("matmul", pb[:],
                                xq[:, dc * TOWN + (tt // 4) * CH + (tt % 4) * 128:
                                   dc * TOWN + (tt // 4) * CH + (tt % 4 + 1) * 128],
                                kn[ki][:, dc * CH:(dc + 1) * CH],
                                start=(dc == 0), stop=False, inc=False)
                    for nt in range(4):
                        pe.call("matmul", pb[:],
                                sb_axg[:, nt * TOWN + (tt // 4) * CH + (tt % 4) * 128:
                                       nt * TOWN + (tt // 4) * CH + (tt % 4 + 1) * 128],
                                ballb[ki][:, nt * CH:(nt + 1) * CH],
                                start=False, stop=False, inc=False)
                    pe.call("matmul", pb[:],
                            sb_axs[:, (tt // 4) * CH + (tt % 4) * 128:
                                   (tt // 4) * CH + (tt % 4 + 1) * 128],
                            bsb[ki][:, :],
                            start=False, stop=True)
                    pv = pe.ctr.v
                    kdep[ki] = pv
                    bldep[ki] = pv
                    ve.wait(s_pe, pv)
                    ve.waitd(av_)
                    ve.waitd(s2dep[par])
                    v1 = ve.call("tensor_add", st1[par][:], pb[:], at[api][:])
                    pdep[par] = v1
                    atdep[api] = v1
                    ve.waitd(xcv)
                    ve.wait(s_ve, v1)
                    v2 = ve.call("tensor_add", st1[par][:], st1[par][:],
                                 xc[:, (tt % 4) * CH:(tt % 4 + 1) * CH])
                    ve.wait(s_ve, v2)
                    v3 = ve.call("tensor_add", st1[par][:], st1[par][:],
                                 bias[:, oc * CH:(oc + 1) * CH])
                    xcdep = v3
                    s1dep[par] = v3
                    sy.wait(s_ve, v3)
                    sv = sy.dma(out=o_out[tt * 128:(tt + 1) * 128,
                                          oc * CH:(oc + 1) * CH],
                                in_=st1[par][:])
                    s2dep[par] = sv



    f_kn0 = e(nc.sbuf_tensor("f_k0", [128, 32 * CH], BF16))
    f_bl0 = e(nc.sbuf_tensor("f_bl0", [128, 4 * CH], BF16))
    f_bs0 = e(nc.sbuf_tensor("f_bs0", [RANK, CH], BF16))
    fdeps = []

    def f_prefetch():
        fdeps.append(gd.dma(out=f_kn0[:].rearrange("p (a o) -> p a o", a=32),
                            in_=fview(i_kern)[:, :, 0:CH]))
        fdeps.append(gd.dma(
            out=f_bl0[:].rearrange("p (a o) -> p a o", a=4),
            in_=i_ball.rearrange("(a p) o -> p a o", p=128)[:, :, 0:CH]))
        fdeps.append(gd.dma(out=f_bs0[:], in_=i_bsT[:, 0:CH]))

    with ExitStack() as ps67:
        pp = ps67.enter_context
        sb_attn = pp(nc.sbuf_tensor("p_attn", [128, NH * TOWN], BF16))
        wo = [pp(nc.sbuf_tensor(f"w_w{i}", [128, 32 * CH], BF16)) for i in range(2)]

        def attention():
            with ExitStack() as ps:
                p2 = ps.enter_context
                sb_cm4 = p2(nc.sbuf_tensor("p_cm4", [128, 4 * CH], BF16))
                gd.waitd(syf_q)
                cmld = gd.dma(out=sb_cm4[:], in_=i_cm4)
                ve.waitd(cmld)
                qsb = [p2(nc.sbuf_tensor(f"a_q{i}", [128, 2 * CH], BF16)) for i in range(2)]
                kb = [p2(nc.sbuf_tensor(f"a_k{i}", [128, CH], BF16)) for i in range(3)]
                vb = [p2(nc.sbuf_tensor(f"a_v{i}", [128, 4 * HD], BF16)) for i in range(3)]
                psb = [p2(nc.sbuf_tensor(f"a_p{i}", [128, 2 * CH], BF16)) for i in range(4)]
                prow = [p2(nc.sbuf_tensor(f"a_w{i}", [128, CH], BF16)) for i in range(4)]
                rcs0 = p2(nc.sbuf_tensor("a_rc0", [1, CH], F32))
                rcs = [rcs0, rcs0]
                bcs0 = p2(nc.sbuf_tensor("a_bc0", [128, CH], F32))
                bcs = [bcs0, bcs0]
                p_sc = [p2(nc.psum_tensor(f"a_s{i}", [128, 2 * CH], F32)) for i in range(2)]
                p_av = [p2(nc.psum_tensor(f"a_av{i}", [128, CH], F32)) for i in range(2)]
                p_rw = [p2(nc.psum_tensor(f"a_rw{i}", [128, CH], F32)) for i in range(2)]

                blocks = [(kvh, duo, qb)
                          for kvh in range(KVH) for duo in range(2) for qb in range(2)]
                sched = []
                for bi, (kvh, duo, qb) in enumerate(blocks):
                    vis = [v for v in range(12) if VISITS[v][0] == qb]
                    for vi, v in enumerate(vis):
                        sched.append((bi, vi, v))
                NG = len(sched)

                qdep = {}
                qld = {}
                kdep = [0, 0, 0]
                vdep = [0, 0, 0]
                scdep = [0, 0]
                pbdep = [0, 0, 0, 0]
                prdep = [0, 0, 0, 0]
                exv = {}
                exk = {}
                avdep = [0, 0]
                rwdep = [0, 0]
                rcsd = [0]
                bcsd = [0]
                kvst = {}

                def load_kv(kvh, pos, par):
                    gd.wait(s_pe, kdep[par])
                    kl = [gd.dma(out=kb[par][0:64, :],
                                 in_=sc_k[kvh // 2, pos, 0,
                                          64 * (kvh % 2):64 * (kvh % 2) + 64, :]),
                          gd.dma(out=kb[par][64:128, :],
                                 in_=sc_k[kvh // 2, pos, 1,
                                          64 * (kvh % 2):64 * (kvh % 2) + 64, :])]
                    gd.wait(s_pe, vdep[par])
                    vl = gd.dma(out=vb[par][:].rearrange("p (a d) -> p a d", a=4),
                                in_=sc_v[kvh, pos].rearrange("(a p) d -> p a d", p=128))
                    return kl, vl

                def load_q(bi):
                    kvh, duo, qb = blocks[bi]
                    qpar = bi % 2
                    qh = (4 * kvh + 2 * duo, 4 * kvh + 2 * duo + 1)
                    for z in range(2):
                        h = qh[z]
                        gd.wait(s_pe, qdep.get((z, qpar), 0))
                        qld[(z, qpar)] = [
                            gd.dma(out=qsb[z][0:64, qpar * CH:(qpar + 1) * CH],
                                   in_=sc_q[2 * (h // 2),
                                            64 * (h % 2):64 * (h % 2) + 64,
                                            qb * CH:(qb + 1) * CH]),
                            gd.dma(out=qsb[z][64:128, qpar * CH:(qpar + 1) * CH],
                                   in_=sc_q[2 * (h // 2) + 1,
                                            64 * (h % 2):64 * (h % 2) + 64,
                                            qb * CH:(qb + 1) * CH])]

                def scores(z, g, v, par, qpar, kl):
                    zz = 2 * z + g
                    pe.waitd(kl)
                    pe.waitd(qld[(z, qpar)])
                    pe.wait(s_ac, scdep[z])
                    for t in range(2):
                        pe.call("matmul", p_sc[z][:, t * CH:(t + 1) * CH],
                                kb[par][:, (2 * g + t) * 128:(2 * g + t + 1) * 128],
                                qsb[z][:, qpar * CH:(qpar + 1) * CH],
                                start=True, stop=True, inc=(t == 1))
                    spv = pe.ctr.v
                    qdep[(z, qpar)] = spv
                    ac.wait(s_pe, spv)
                    ac.wait(s_pe, pbdep[zz])
                    a = ac.call("activation", psb[zz][:], p_sc[z][:], AF.Exp,
                                scale=SCALE, bias=sb_btab[:, v:v + 1])
                    scdep[z] = a
                    if v in DIAG:
                        ve.wait(s_ac, a)
                        mv = 0
                        for t in range(2):
                            kt = 2 * g + t
                            mv = ve.call("tensor_mul",
                                         psb[zz][:, t * CH:(t + 1) * CH],
                                         psb[zz][:, t * CH:(t + 1) * CH],
                                         sb_cm4[:, kt * CH:(kt + 1) * CH])
                        exv[(z, g)] = mv
                        exk[(z, g)] = "ve"
                        ve.wait(s_ve, mv)
                    else:
                        exv[(z, g)] = a
                        exk[(z, g)] = "ac"
                        ve.wait(s_ac, a)
                    ve.wait(s_pe, prdep[zz])
                    pav = ve.call("tensor_add", prow[zz][:],
                                  psb[zz][:, 0:CH], psb[zz][:, CH:2 * CH])
                    exv[(z, g, "row")] = pav

                def avrows(z, g, par, vl, first, last):
                    zz = 2 * z + g
                    if exk[(z, g)] == "ve":
                        pe.wait(s_ve, exv[(z, g)])
                    else:
                        pe.wait(s_ac, exv[(z, g)])
                    pe.waitd(vl)
                    if first:
                        pe.wait(s_ve, avdep[z])
                        pe.wait(s_ve, rwdep[z])
                    for t in range(2):
                        kt = 2 * g + t
                        pe.call("matmul", p_av[z][:],
                                vb[par][:, kt * 128:(kt + 1) * 128],
                                psb[zz][:, t * CH:(t + 1) * CH],
                                start=(first and t == 0), stop=(last and t == 1),
                                inc=False)
                    pe.wait(s_ve, exv[(z, g, "row")])
                    pe.call("matmul", p_rw[z][0:1, :], sb_onesb[:, 0:1],
                            prow[zz][:], start=first, stop=last)
                    pbdep[zz] = pe.ctr.v
                    prdep[zz] = pe.ctr.v
                    kdep[par] = pe.ctr.v
                    vdep[par] = pe.ctr.v

                def finalize(bi):
                    kvh, duo, qb = blocks[bi]
                    qh = (4 * kvh + 2 * duo, 4 * kvh + 2 * duo + 1)
                    for z in range(2):
                        h = qh[z]
                        ve.wait(s_pe, pbdep[2 * z])
                        ve.wait(s_pe, pbdep[2 * z + 1])
                        ve.wait(s_pe, rcsd[0])
                        rv = ve.call("reciprocal", rcs[z][0:1, :], p_rw[z][0:1, :])
                        rwdep[z] = rv
                        pe.wait(s_ve, rv)
                        bv = pe.call("matmul", p_rw[z][:, :], sb_onesf[0:1, :],
                                     rcs[z][0:1, :], start=True, stop=True)
                        rcsd[0] = bv
                        ve.wait(s_pe, bv)
                        ve.wait(s_ve, bcsd[0])
                        cv = ve.call("tensor_copy", bcs[z][:], p_rw[z][:, :])
                        rwdep[z] = cv
                        ve.wait(s_ve, cv)
                        nv2 = ve.call("tensor_mul",
                                      sb_attn[:, h * TOWN + qb * CH:
                                              h * TOWN + (qb + 1) * CH],
                                      p_av[z][:], bcs[z][:])
                        avdep[z] = nv2
                        bcsd[0] = nv2

                load_q(0)
                kvst[0] = load_kv(blocks[0][0], VISITS[sched[0][2]][1], 0)
                for gi, (bi, vi, v) in enumerate(sched):
                    qpar = bi % 2
                    par = gi % 3
                    kl, vl = kvst[gi]
                    if gi + 1 < NG:
                        nbi, nvi, nv_ = sched[gi + 1]
                        kvst[gi + 1] = load_kv(blocks[nbi][0], VISITS[nv_][1],
                                               (gi + 1) % 3)
                        if nvi == 0:
                            load_q(nbi)
                    scores(0, 0, v, par, qpar, kl)
                    scores(1, 0, v, par, qpar, kl)
                    if gi > 0:
                        pvl = kvst[gi - 1][1]
                        ppar = (gi - 1) % 3
                        avrows(0, 1, ppar, pvl, False, vi == 0)
                        avrows(1, 1, ppar, pvl, False, vi == 0)
                    if vi == 0 and bi > 0:
                        finalize(bi - 1)
                    scores(0, 1, v, par, qpar, kl)
                    scores(1, 1, v, par, qpar, kl)
                    avrows(0, 0, par, vl, vi == 0, False)
                    avrows(1, 0, par, vl, vi == 0, False)
                gl = NG - 1
                avrows(0, 1, gl % 3, kvst[gl][1], False, True)
                avrows(1, 1, gl % 3, kvst[gl][1], False, True)
                finalize(len(blocks) - 1)

        attention()
        gdf = gd.frontier()
        gv0 = gd.dma(out=wo[0][:].rearrange("p (a o) -> p a o", a=32),
                     in_=fview(i_woT)[:, :, 0:CH])
        barrier(gd_front=gdf)

        def wo_phase(gv0, pf_hook):
            with ExitStack() as ps:
                p2 = ps.enter_context
                stg = [p2(nc.sbuf_tensor(f"w_s{i}", [128, CH], BF16)) for i in range(2)]
                pw = [p2(nc.psum_tensor(f"w_p{i}", [128, CH], F32)) for i in range(2)]
                wdep = [0, 0]
                pdep = [0, 0]
                sdep = [None, None]
                gv = [gv0, None]
                for dc in range(8):
                    wi = dc % 2
                    if dc + 1 < 8:
                        ni = (dc + 1) % 2
                        gd.wait(s_pe, wdep[ni])
                        gv[ni] = gd.dma(
                            out=wo[ni][:].rearrange("p (a o) -> p a o", a=32),
                            in_=fview(i_woT)[:, :, (dc + 1) * CH:(dc + 2) * CH])
                    if dc == 2:
                        pf_hook()
                    pe.waitd(gv[wi])
                    for tt in range(8):
                        par = tt % 2
                        pb = pw[par]
                        pe.wait(s_ve, pdep[par])
                        for oc in range(32):
                            qb, tl = tt // 4, tt % 4
                            pe.call("matmul", pb[:],
                                    sb_attn[:, oc * TOWN + qb * CH + tl * 128:
                                            oc * TOWN + qb * CH + (tl + 1) * 128],
                                    wo[wi][:, oc * CH:(oc + 1) * CH],
                                    start=(oc == 0), stop=(oc == 31), inc=(oc == 31))
                        pv = pe.ctr.v
                        wdep[wi] = pv
                        ve.wait(s_pe, pv)
                        ve.waitd(sdep[par])
                        cv = ve.call("tensor_copy", stg[par][:], pb[:])
                        pdep[par] = cv
                        sy.wait(s_ve, cv)
                        sv = sy.dma(out=sc_at[tt * 128:(tt + 1) * 128,
                                              dc * CH:(dc + 1) * CH],
                                    in_=stg[par][:])
                        sdep[par] = sv

        wo_phase(gv0, f_prefetch)
    barrier()
    final_phase(f_kn0, f_bl0, f_bs0, fdeps)
    for sem, v in sy.frontier():
        sy.wait(sem, v)

    with nc.Block() as block:

        @block.gpsimd
        def _(g):
            for f in gd.plan:
                f(g)

        @block.sync
        def _(s):
            for f in sy.plan:
                f(s)

        @block.tensor
        def _(t):
            for f in pe.plan:
                f(t)

        @block.vector
        def _(v):
            for f in ve.plan:
                f(v)

        @block.scalar
        def _(s):
            for f in ac.plan:
                f(s)

    stack.close()
    return nc


def _rope_tables():
    inv = 1.0 / (500000.0 ** (np.arange(0, HD, 2, dtype=np.float64) / HD))
    t = np.arange(S, dtype=np.float64)
    ang = np.outer(inv, t)
    cos = np.cos(ang).astype(np.float32)
    sin = np.sin(ang).astype(np.float32)
    cos2 = np.concatenate([cos, cos], 0)
    sin2 = np.concatenate([sin, sin], 0)
    return cos2, sin2


def _half_perm(nheads):
    perm = []
    for p in range(nheads // 2):
        for half in range(2):
            for h in (2 * p, 2 * p + 1):
                base = h * HD + half * 64
                perm.extend(range(base, base + 64))
    return np.asarray(perm)


_NC_CACHE = {}


def prepare_in_maps(L0_attn_x, L0_attn_w_qkv, L0_attn_w_o, L0_ffn_gate,
                    L0_ffn_kernel, L0_ffn_bias, L0_ffn_lora_A, L0_ffn_lora_B,
                    L0_ffn_lora_A_shared, L0_ffn_lora_B_shared):
    x = np.ascontiguousarray(np.asarray(L0_attn_x, dtype=np.float32))
    w_qkv = np.asarray(L0_attn_w_qkv, dtype=np.float32)
    w_o = np.asarray(L0_attn_w_o, dtype=np.float32)
    gate = np.asarray(L0_ffn_gate, dtype=np.float32)
    kern = np.asarray(L0_ffn_kernel, dtype=np.float32)
    biasv = np.asarray(L0_ffn_bias, dtype=np.float32)
    lA = np.asarray(L0_ffn_lora_A, dtype=np.float32)
    lB = np.asarray(L0_ffn_lora_B, dtype=np.float32)
    lAs = np.asarray(L0_ffn_lora_A_shared, dtype=np.float32)
    lBs = np.asarray(L0_ffn_lora_B_shared, dtype=np.float32)

    permq = _half_perm(NH)
    permk = _half_perm(KVH)
    wqT = np.ascontiguousarray(w_qkv[:D].T[:, permq])
    wkT = np.ascontiguousarray(w_qkv[D:D + KVH * HD].T[:, permk])
    wvT = np.ascontiguousarray(w_qkv[D + KVH * HD:].T)
    woT = np.ascontiguousarray(w_o.T)
    biasT = np.ascontiguousarray(np.broadcast_to(biasv, (128, D)))
    aallT = np.ascontiguousarray(lA.transpose(2, 0, 1).reshape(D, NRTOT))
    ball = np.ascontiguousarray(lB.transpose(0, 2, 1).reshape(NRTOT, D))
    asT = np.ascontiguousarray(lAs.T)
    bsT = np.ascontiguousarray(lBs.T)
    cos2, sin2 = _rope_tables()

    cm4 = np.zeros((128, 4 * CH), dtype=bfdt)
    for kt in range(4):
        kvloc = kt * 128 + np.arange(128)[:, None]
        cm4[:, kt * CH:(kt + 1) * CH] = (kvloc <= np.arange(CH)[None, :]).astype(bfdt)
    onesf = np.ones((1, 128), np.float32)
    onesb = np.ones((128, 1), bfdt)

    in_maps = []
    slices = []
    for c in range(8):
        b, j = c // 4, c % 4
        A_real, B_real = j, 7 - j
        place = [-1] * NCHUNK
        place[3] = A_real
        place[7] = B_real
        for i in range(j):
            place[i] = i
        rest = [cchunk for cchunk in range(j + 1, 7 - j)]
        free = [p for p in range(7) if place[p] < 0]
        for p, cchunk in zip(free, rest):
            place[p] = cchunk
        leftover = [cchunk for cchunk in range(NCHUNK) if cchunk not in place]
        li = 0
        for p in range(NCHUNK):
            if place[p] < 0:
                place[p] = leftover[li]
                li += 1
        kvtok = np.concatenate([np.arange(place[p] * CH, (place[p] + 1) * CH)
                                for p in range(NCHUNK)])
        owntok = np.concatenate([np.arange(A_real * CH, (A_real + 1) * CH),
                                 np.arange(B_real * CH, (B_real + 1) * CH)])
        xb = x[b].reshape(S, D)
        xT = xb.T
        xTkv = np.ascontiguousarray(xT[:, kvtok])
        xown = np.ascontiguousarray(xb[owntok])

        btab = np.zeros((128, 12), np.float32)
        for v, (qb, p) in enumerate(VISITS):
            if v in DIAG:
                btab[:, v] = -SHIFT
            else:
                qc = A_real if qb == 0 else B_real
                btab[:, v] = -SHIFT if place[p] < qc else EMPTY

        lbA = b * 4 + A_real // 2
        lbB = b * 4 + B_real // 2
        gnr = np.zeros((128, 8), np.float32)
        p_idx = np.arange(128)
        for tc, lb in ((0, lbA), (1, lbB)):
            for nt in range(4):
                gnr[:, tc * 4 + nt] = gate[lb, 4 * nt + p_idx // 32]

        in_maps.append({
            "xTkv": xTkv, "xown": xown,
            "wqT": wqT, "wkT": wkT, "wvT": wvT, "woT": woT,
            "kern": kern, "biasT": biasT,
            "aallT": aallT, "ball": ball, "asT": asT, "bsT": bsT,
            "gatenr": gnr,
            "cos_kv": np.ascontiguousarray(cos2[:, kvtok]),
            "sin_kv": np.ascontiguousarray(sin2[:, kvtok]),
            "cos_q": np.ascontiguousarray(cos2[:, owntok]),
            "sin_q": np.ascontiguousarray(sin2[:, owntok]),
            "cmask4": cm4, "btab": btab,
            "onesf": onesf, "onesb": onesb, "onesbr": np.ones((1, 128), bfdt),
            "epsc": np.full((128, 1), EPS, np.float32),
        })
        slices.append((b, owntok))
    return in_maps, slices


def kernel(**inputs):
    in_maps, slices = prepare_in_maps(**inputs)
    if "nc" not in _NC_CACHE:
        _NC_CACHE["nc"] = build_program()
    nc = _NC_CACHE["nc"]
    res = run_bass_kernel_spmd(nc, in_maps, core_ids=list(range(8)))
    out = np.empty((2, S, D), dtype=np.float32)
    for c in range(8):
        b, owntok = slices[c]
        out[b, owntok, :] = res.results[c]["out"]
    return out.reshape(2, S, D)




# version 17
# speedup vs baseline: 1.1931x; 1.0219x over previous
import numpy as np
import ml_dtypes
import concourse.bass as bass
import concourse.mybir as mybir
from concourse.bass_utils import run_bass_kernel_spmd

F32 = mybir.dt.float32
BF16 = mybir.dt.bfloat16
AF = mybir.ActivationFunctionType
ALU = mybir.AluOpType
bfdt = ml_dtypes.bfloat16

D = 4096
S = 4096
NH = 32
KVH = 8
HD = 128
CH = 512
NCHUNK = 8
TOWN = 1024
RANK = 32
NLORA = 16
NRTOT = NLORA * RANK
SHIFT = 12.0
EMPTY = -30000.0
SCALE = float(1.0 / np.sqrt(128.0))
EPS = 1e-5

VISITS = [(0, p) for p in range(4)] + [(1, p) for p in range(8)]
DIAG = {3, 11}


class Ctr:
    def __init__(self, step=1):
        self.v = 0
        self.step = step

    def tick(self):
        self.v += self.step
        return self.v


class Rec:

    def __init__(self, sem, ctr):
        self.sem = sem
        self.ctr = ctr
        self.plan = []
        self.marks = {}

    def wait(self, sem, val):
        if val is None or val <= 0:
            return
        k = id(sem)
        if self.marks.get(k, 0) >= val:
            return
        self.marks[k] = val

        def f(eng, s=sem, v=val):
            eng.wait_ge(s, v)

        self.plan.append(f)

    def waitd(self, dep):
        if dep is None:
            return
        if isinstance(dep, list):
            for d in dep:
                self.waitd(d)
            return
        self.wait(dep[0], dep[1])

    def frontier(self):
        return [(self.sem, self.ctr.v)]

    def call(self, meth, *a, inc=True, **kw):
        if inc:
            v = self.ctr.tick()

            def f(eng, m=meth, aa=a, kk=kw, s=self.sem, st=self.ctr.step):
                getattr(eng, m)(*aa, **kk).then_inc(s, st)
        else:
            v = self.ctr.v

            def f(eng, m=meth, aa=a, kk=kw):
                getattr(eng, m)(*aa, **kk)

        self.plan.append(f)
        return v


class DmaRec(Rec):

    def __init__(self, sems):
        super().__init__(sems[0], Ctr(16))
        self.sems = sems
        self.ctrs = [Ctr(16) for _ in sems]
        self.rr = 0

    def frontier(self):
        return [(s, c.v) for s, c in zip(self.sems, self.ctrs)]

    def dma(self, **kw):
        i = self.rr % len(self.sems)
        self.rr += 1
        sem, ctr = self.sems[i], self.ctrs[i]
        self.wait(sem, ctr.v)
        v = ctr.tick()

        def f(eng, kk=kw, s=sem, vv=v):
            eng.dma_start(**kk).then_inc(s, 16)

        self.plan.append(f)
        return (sem, v)


def build_program():
    nc = bass.Bass("TRN2", target_bir_lowering=False, debug=False)
    dram = nc.dram_tensor

    i_xTkv = dram("xTkv", [D, S], F32, kind="ExternalInput").ap()
    i_xown = dram("xown", [TOWN, D], F32, kind="ExternalInput").ap()
    i_wqT = dram("wqT", [D, D], F32, kind="ExternalInput").ap()
    i_wkT = dram("wkT", [D, KVH * HD], F32, kind="ExternalInput").ap()
    i_wvT = dram("wvT", [D, KVH * HD], F32, kind="ExternalInput").ap()
    i_woT = dram("woT", [D, D], F32, kind="ExternalInput").ap()
    i_kern = dram("kern", [D, D], F32, kind="ExternalInput").ap()
    i_bias = dram("biasT", [128, D], F32, kind="ExternalInput").ap()
    i_aall = dram("aallT", [D, NRTOT], F32, kind="ExternalInput").ap()
    i_ball = dram("ball", [NRTOT, D], F32, kind="ExternalInput").ap()
    i_asT = dram("asT", [D, RANK], F32, kind="ExternalInput").ap()
    i_bsT = dram("bsT", [RANK, D], F32, kind="ExternalInput").ap()
    i_gnr = dram("gatenr", [128, 8], F32, kind="ExternalInput").ap()
    i_cos_kv = dram("cos_kv", [128, S], F32, kind="ExternalInput").ap()
    i_sin_kv = dram("sin_kv", [128, S], F32, kind="ExternalInput").ap()
    i_cos_q = dram("cos_q", [128, TOWN], F32, kind="ExternalInput").ap()
    i_sin_q = dram("sin_q", [128, TOWN], F32, kind="ExternalInput").ap()
    i_cm4 = dram("cmask4", [128, 4 * CH], BF16, kind="ExternalInput").ap()
    i_btab = dram("btab", [128, 12], F32, kind="ExternalInput").ap()
    i_onesf = dram("onesf", [1, 128], F32, kind="ExternalInput").ap()
    i_onesb = dram("onesb", [128, 1], BF16, kind="ExternalInput").ap()
    i_onesbr = dram("onesbr", [1, 128], BF16, kind="ExternalInput").ap()
    i_epsc = dram("epsc", [128, 1], F32, kind="ExternalInput").ap()
    o_out = dram("out", [TOWN, D], F32, kind="ExternalOutput").ap()

    sc_xkv = dram("sc_xkv", [D, S], BF16, kind="Internal").ap()
    sc_q = dram("sc_q", [NH, 128, TOWN], BF16, kind="Internal").ap()
    sc_k = dram("sc_k", [KVH // 2, NCHUNK, 2, 128, CH], BF16, kind="Internal").ap()
    sc_v = dram("sc_v", [KVH, NCHUNK, CH, HD], BF16, kind="Internal").ap()
    sc_at = dram("sc_at", [TOWN, D], BF16, kind="Internal").ap()

    from contextlib import ExitStack
    stack = ExitStack()
    e = stack.enter_context

    NQ = 5
    gd_sems = [e(nc.semaphore(f"s_gd{i}")) for i in range(NQ)]
    sy_sems = [e(nc.semaphore(f"s_sy{i}")) for i in range(NQ)]
    s_pe = e(nc.semaphore("s_pe"))
    s_ve = e(nc.semaphore("s_ve"))
    s_ac = e(nc.semaphore("s_ac"))
    s_bar = e(nc.semaphore("s_bar"))

    gd = DmaRec(gd_sems)
    sy = DmaRec(sy_sems)
    pe = Rec(s_pe, Ctr())
    ve = Rec(s_ve, Ctr())
    ac = Rec(s_ac, Ctr())
    bar_n = [0]

    def barrier(gd_front=None, sy_front=None, skip_gd=False):
        for r in (pe, ve, ac):
            for sem, v in r.frontier():
                sy.wait(sem, v)
        for sem, v in (gd_front if gd_front is not None else gd.frontier()):
            sy.wait(sem, v)
        for sem, v in (sy_front if sy_front is not None else sy.frontier()):
            sy.wait(sem, v)
        bar_n[0] += 1

        def f(eng):
            eng.sem_inc(s_bar, 1)

        sy.plan.append(f)
        sy.marks[id(s_bar)] = bar_n[0]
        for r in ((pe, ve, ac) if skip_gd else (pe, ve, ac, gd)):
            r.wait(s_bar, bar_n[0])

    sb_onesf = e(nc.sbuf_tensor("p_onesf", [1, 128], F32))
    sb_onesb = e(nc.sbuf_tensor("p_onesb", [128, 1], BF16))
    sb_onesbr = e(nc.sbuf_tensor("p_onesbr", [1, 128], BF16))
    sb_btab = e(nc.sbuf_tensor("p_btab", [128, 12], F32))
    sb_gnr = e(nc.sbuf_tensor("p_gnr", [128, 8], F32))
    sb_cpool = e(nc.sbuf_tensor("p_cpool", [128, 1], F32))
    sb_axg = e(nc.sbuf_tensor("p_axg", [128, 4 * TOWN], BF16))
    sb_axs = e(nc.sbuf_tensor("p_axs", [RANK, TOWN], BF16))
    nc.const_aps.register(F32, EPS, sb_cpool[:, 0:1])

    g_base = []
    for dst, src in ((sb_onesf, i_onesf), (sb_onesb, i_onesb), (sb_onesbr, i_onesbr),
                     (sb_btab, i_btab), (sb_gnr, i_gnr), (sb_cpool, i_epsc)):
        g_base.append(gd.dma(out=dst[:], in_=src))

    def fview(t):
        return t.rearrange("(a p) t -> p a t", p=128)

    def ln_pass(xT_in, ntok, sc_out, tag, hook=None, post_hook=None):
        nch = ntok // CH
        SLAB = 4 * CH
        with ExitStack() as ps:
            pp = ps.enter_context
            xch = [pp(nc.sbuf_tensor(f"l{tag}_x{i}", [128, 32 * CH], BF16)) for i in range(3)]
            sq = [pp(nc.sbuf_tensor(f"l{tag}_q{i}", [128, CH], BF16)) for i in range(2)]
            tb = [pp(nc.sbuf_tensor(f"l{tag}_t{i}", [128, CH], BF16)) for i in range(2)]
            srow = [pp(nc.sbuf_tensor(f"l{tag}_s{i}", [1, CH], F32)) for i in range(2)]
            qrow = [pp(nc.sbuf_tensor(f"l{tag}_ss{i}", [1, CH], F32)) for i in range(2)]
            rrow = pp(nc.sbuf_tensor(f"l{tag}_r", [1, CH], F32))
            nrow = [pp(nc.sbuf_tensor(f"l{tag}_n{i}", [1, CH], BF16)) for i in range(2)]
            rbrow = [pp(nc.sbuf_tensor(f"l{tag}_rb{i}", [1, CH], BF16)) for i in range(2)]
            bcm = pp(nc.sbuf_tensor(f"l{tag}_bm", [128, CH], BF16))
            bcr = pp(nc.sbuf_tensor(f"l{tag}_br", [128, CH], BF16))
            p_sum = pp(nc.psum_tensor(f"l{tag}_ps", [1, CH], F32))
            p_sq = pp(nc.psum_tensor(f"l{tag}_pq", [1, CH], F32))
            p_bc = pp(nc.psum_tensor(f"l{tag}_pb", [128, 2 * CH], F32))
            xload = [None, None, None]
            xfree = [0, 0, 0]
            xttl = [None, None, None]
            xpe = [0, 0, 0]
            xstore = [None, None, None]
            qdep = [0, 0]
            tdep = [0, 0]
            row_done = 0
            rowfin = [0, 0]
            rowpe = [0, 0]
            rowac = [0]
            stats_done = {}

            def do_stats(tc):
                nonlocal row_done
                xi = tc % 3
                ri = tc % 2
                pe.wait(s_ve, row_done)
                for dc in range(32):
                    si = dc % 2
                    pe.waitd(xload[xi][dc // 8])
                    ac.waitd(xload[xi][dc // 8])
                    ac.wait(s_pe, qdep[si])
                    av = ac.call("activation", sq[si][:],
                                 xch[xi][:, dc * CH:(dc + 1) * CH], AF.Square)
                    pe.wait(s_ac, av)
                    pe.call("matmul", p_sum[0:1, :], sb_onesb[:, 0:1],
                            xch[xi][:, dc * CH:(dc + 1) * CH],
                            start=(dc == 0), stop=(dc == 31))
                    pq = pe.call("matmul", p_sq[0:1, :], sb_onesb[:, 0:1],
                                 sq[si][:],
                                 start=(dc == 0), stop=(dc == 31))
                    qdep[si] = pq
                stats_done[tc] = pe.ctr.v
                ve.wait(s_pe, stats_done[tc])
                ve.wait(s_ve, ve.ctr.v)
                ve.call("tensor_copy", srow[ri][0:1, :], p_sum[0:1, :])
                row_done = ve.call("tensor_copy", qrow[ri][0:1, :], p_sq[0:1, :])

            def do_rows(tc):
                ri = tc % 2
                ve.wait(s_ve, ve.ctr.v)
                ve.wait(s_ac, rowac[0])
                m1 = ve.call("tensor_scalar", srow[ri][0:1, :], srow[ri][0:1, :],
                             1.0 / D, None, ALU.mult)
                ve.wait(s_ve, m1)
                v1 = ve.call("tensor_mul", rrow[0:1, :], srow[ri][0:1, :],
                             srow[ri][0:1, :])
                ve.wait(s_ve, v1)
                v2 = ve.call("scalar_tensor_tensor", rrow[0:1, :], qrow[ri][0:1, :],
                             1.0 / D, rrow[0:1, :], ALU.mult, ALU.subtract)
                ac.wait(s_ve, v2)
                ac.waitd(g_base)
                a1 = ac.call("activation", rrow[0:1, :], rrow[0:1, :], AF.Ln, bias=EPS)
                ac.wait(s_ac, a1)
                a2 = ac.call("activation", rrow[0:1, :], rrow[0:1, :], AF.Exp,
                             scale=-0.5)
                rowac[0] = a2

            def do_rows_b(tc):
                ri = tc % 2
                ve.wait(s_ac, rowac[0])
                ve.wait(s_pe, rowpe[ri])
                ve.call("tensor_scalar", nrow[ri][0:1, :], srow[ri][0:1, :], -1.0,
                        None, ALU.mult)
                n2 = ve.call("tensor_copy", rbrow[ri][0:1, :], rrow[0:1, :])
                rowfin[ri] = n2

            def do_tts(tc):
                xi = tc % 3
                ri = tc % 2
                pe.wait(s_ve, rowfin[ri])
                pe.call("matmul", p_bc[:, 0:CH], sb_onesbr[0:1, :],
                        nrow[ri][0:1, :], start=True, stop=True)
                pb = pe.call("matmul", p_bc[:, CH:2 * CH], sb_onesbr[0:1, :],
                             rbrow[ri][0:1, :], start=True, stop=True)
                rowpe[ri] = pb
                ve.wait(s_pe, pb)
                ve.call("tensor_copy", bcm[:], p_bc[:, 0:CH])
                bcv = ve.call("tensor_copy", bcr[:], p_bc[:, CH:2 * CH])
                ve.wait(s_ve, bcv)
                last_tt = 0
                stores = []
                ttl = []
                for dc in range(32):
                    ti = dc % 2
                    ve.wait(s_ve, tdep[ti])
                    xs = xch[xi][:, dc * CH:(dc + 1) * CH]
                    v1 = ve.call("tensor_add", tb[ti][:], xs, bcm[:])
                    ve.wait(s_ve, v1)
                    v2 = ve.call("tensor_mul", xs, tb[ti][:], bcr[:])
                    tdep[ti] = v2
                    last_tt = v2
                    ttl.append(v2)
                    if dc % 2 == 1:
                        sy.wait(s_ve, v2)
                        stores.append(sy.dma(
                            out=fview(sc_out)[:, dc - 1:dc + 1,
                                              tc * CH:(tc + 1) * CH],
                            in_=xch[xi][:, (dc - 1) * CH:(dc + 1) * CH].rearrange(
                                "p (a t) -> p a t", a=2)))
                xfree[xi] = last_tt
                xttl[xi] = ttl
                xstore[xi] = stores

            def load_chunk(tc):
                xi = tc % 3
                gd.wait(s_ve, xfree[xi])
                gd.wait(s_pe, xpe[xi])
                gd.waitd(xstore[xi])
                deps = []
                for pc in range(4):
                    deps.append(gd.dma(
                        out=xch[xi][:].rearrange("p (a t) -> p a t", a=32)[
                            :, pc * 8:(pc + 1) * 8, :],
                        in_=fview(xT_in)[:, pc * 8:(pc + 1) * 8,
                                         tc * CH:(tc + 1) * CH]))
                xload[xi] = deps

            load_chunk(0)
            for tc in range(nch):
                if hook is not None:
                    hook(tc)
                do_stats(tc)
                do_rows(tc)
                if tc > 0:
                    do_tts(tc - 1)
                    if post_hook is not None:
                        xpe[(tc - 1) % 3] = post_hook(tc - 1, xch[(tc - 1) % 3],
                                                      xttl[(tc - 1) % 3])
                do_rows_b(tc)
                if tc + 1 < nch:
                    load_chunk(tc + 1)
            do_tts(nch - 1)
            if post_hook is not None:
                xpe[(nch - 1) % 3] = post_hook(nch - 1, xch[(nch - 1) % 3],
                                               xttl[(nch - 1) % 3])

    def rope_block(b0, b1, cs, sn, bufs, dst0, dst1):
        er, orr, t1, t2, ke, ko = bufs
        ve.call("tensor_copy", er[:], b0[:])
        c2 = ve.call("tensor_copy", orr[:], b1[:])
        ve.wait(s_ve, c2)
        ve.call("tensor_mul", t1[:], er[:], cs)
        v2 = ve.call("scalar_tensor_tensor", t2[:], orr[:], -1.0, sn, ALU.mult, ALU.mult)
        ve.wait(s_ve, v2)
        v3 = ve.call("tensor_add", ke[:], t1[:], t2[:])
        ve.wait(s_ve, v3)
        ve.call("tensor_mul", t1[:], orr[:], cs)
        v5 = ve.call("tensor_mul", t2[:], er[:], sn)
        ve.wait(s_ve, v5)
        v6 = ve.call("tensor_add", ko[:], t1[:], t2[:])
        sy.wait(s_ve, v3)
        s1 = sy.dma(out=dst0, in_=ke[:])
        sy.wait(s_ve, v6)
        s2 = sy.dma(out=dst1, in_=ko[:])
        return c2, v6, [s1, s2]

    def kv_sweeps(wsb, kscr, pk, xp, xqh0, xqh1, wv_deps, xv0):
        with ExitStack() as ps:
            pv4 = [ps.enter_context(nc.psum_tensor(f"v_p{i}", [128, CH], F32))
                   for i in range(4)]
            bufs = [xp[0], xqh0, xqh1]
            idxmap = [0, 2, 0, 1, 0, 2, 0, 2]
            xdep = [0, 0, 0]
            xv = [None] * NCHUNK
            xv[0] = xv0
            sdep = [None, None]
            vstage = [kscr[4], kscr[5]]
            pdep = [0, 0]

            def loadx(tc):
                bi = idxmap[tc]
                sy.wait(s_pe, xdep[bi])
                xv[tc] = sy.dma(out=bufs[bi][:],
                                in_=fview(sc_xkv)[:, :, tc * CH:(tc + 1) * CH])

            loadx(1)
            pall = [pk[0], pk[1], pk[2], pk[3], pv4[0], pv4[1], pv4[2], pv4[3]]
            for dc in range(32):
                if dc % 8 == 0:
                    pe.waitd(xv0[dc // 8])
                if dc % 4 == 0:
                    pe.waitd(wv_deps[dc // 4])
                for b in range(8):
                    tt, vh = b // 2, b % 2
                    pe.call("matmul", pall[b][:],
                            bufs[0][:, dc * CH + tt * 128:dc * CH + (tt + 1) * 128],
                            wsb[:, dc * 1024 + vh * CH:dc * 1024 + (vh + 1) * CH],
                            start=(dc == 0), stop=(dc == 31),
                            inc=(dc == 31 and b == 7))
            pv = pe.ctr.v
            xdep[0] = pv
            for b in range(8):
                tt, vh = b // 2, b % 2
                par = b % 2
                ve.wait(s_pe, pv)
                ve.waitd(sdep[par])
                cv = ve.call("tensor_copy", vstage[par][:], pall[b][:])
                if b < 2:
                    pdep[b] = cv
                sy.wait(s_ve, cv)
                svl = []
                for hh in range(4):
                    h = vh * 4 + hh
                    svl.append(sy.dma(
                        out=sc_v[h, 0, tt * 128:(tt + 1) * 128, :],
                        in_=vstage[par][:, hh * 128:(hh + 1) * 128]))
                sdep[par] = svl
            for tc in range(1, NCHUNK):
                bi = idxmap[tc]
                if tc + 1 < NCHUNK:
                    loadx(tc + 1)
                pe.waitd(xv[tc])
                for tt in range(4):
                    for vh in range(2):
                        par = vh % 2
                        pb = pk[par]
                        pe.wait(s_ve, pdep[par])
                        for dc in range(32):
                            pe.call("matmul", pb[:],
                                    bufs[bi][:, dc * CH + tt * 128:
                                             dc * CH + (tt + 1) * 128],
                                    wsb[:, dc * 1024 + vh * CH:
                                        dc * 1024 + (vh + 1) * CH],
                                    start=(dc == 0), stop=(dc == 31), inc=(dc == 31))
                        pv = pe.ctr.v
                        xdep[bi] = pv
                        ve.wait(s_pe, pv)
                        ve.waitd(sdep[par])
                        cv = ve.call("tensor_copy", vstage[par][:], pb[:])
                        pdep[par] = cv
                        sy.wait(s_ve, cv)
                        svl = []
                        for hh in range(4):
                            h = vh * 4 + hh
                            svl.append(sy.dma(
                                out=sc_v[h, tc, tt * 128:(tt + 1) * 128, :],
                                in_=vstage[par][:, hh * 128:(hh + 1) * 128]))
                        sdep[par] = svl



    def q_proj(xqh0, xqh1, wq0, gq0, pk, kcs, ksn, kscr, aall, asb, lg):
        with ExitStack() as ps:
            pp = ps.enter_context
            wq = [wq0, pp(nc.sbuf_tensor("q_w1", [128, 32 * 128], BF16))]
            xqh = [xqh0, xqh1]
            cd = [gd.dma(out=kcs[t][:], in_=i_cos_q[:, t * CH:(t + 1) * CH])
                  for t in range(2)]
            sd = [gd.dma(out=ksn[t][:], in_=i_sin_q[:, t * CH:(t + 1) * CH])
                  for t in range(2)]
            ve.waitd(cd)
            ve.waitd(sd)
            wdep = [0, 0]
            pdep = [0, 0, 0, 0]
            sdep = None
            rdep = 0
            gv = [gq0, None]
            for pr in range(NH // 2):
                pvt = [0, 0]
                for idx in range(2):
                    hp = 2 * pr + idx
                    wi = hp % 2
                    if hp + 1 < NH:
                        ni = (hp + 1) % 2
                        gd.wait(s_pe, wdep[ni])
                        gv[ni] = gd.dma(
                            out=wq[ni][:].rearrange("p (a o) -> p a o", a=32),
                            in_=fview(i_wqT)[:, :, (hp + 1) * 128:(hp + 2) * 128])
                    if hp == 4:
                        lg.append(gd.dma(out=aall[:].rearrange("p (a r) -> p a r", a=32),
                                         in_=fview(i_aall)))
                        lg.append(gd.dma(out=asb[:].rearrange("p (a r) -> p a r", a=32),
                                         in_=fview(i_asT)))
                    pe.waitd(gv[wi])
                    for tc in range(2):
                        pb = pk[2 * tc + idx]
                        pe.wait(s_ve, pdep[2 * tc + idx])
                        for dc in range(32):
                            pe.call("matmul", pb[:],
                                    wq[wi][:, dc * 128:(dc + 1) * 128],
                                    xqh[tc][:, dc * CH:(dc + 1) * CH],
                                    start=(dc == 0), stop=(dc == 31), inc=(dc == 31))
                        pvt[tc] = pe.ctr.v
                    wdep[wi] = pe.ctr.v
                for tc in range(2):
                    b0, b1 = pk[2 * tc], pk[2 * tc + 1]
                    ve.wait(s_pe, pvt[tc])
                    ve.wait(s_ve, rdep)
                    ve.waitd(sdep)
                    c2, v6, s2 = rope_block(
                        b0, b1, kcs[tc][:], ksn[tc][:], kscr[0:6],
                        sc_q[2 * pr, :, tc * CH:(tc + 1) * CH],
                        sc_q[2 * pr + 1, :, tc * CH:(tc + 1) * CH])
                    pdep[2 * tc] = c2
                    pdep[2 * tc + 1] = c2
                    rdep = v6
                    sdep = s2

    def lora_pre(xqh0, xqh1, aall, asb, lg):
        with ExitStack() as ps:
            pp = ps.enter_context
            pl = pp(nc.psum_tensor("lp_pl", [128, CH], F32))
            pls = pp(nc.psum_tensor("lp_pls", [RANK, CH], F32))
            xqh = [xqh0, xqh1]
            pe.waitd(lg)
            for nt in range(4):
                for tc in range(2):
                    for dc in range(32):
                        pe.call("matmul", pl[:],
                                aall[:, dc * NRTOT + nt * 128:dc * NRTOT + (nt + 1) * 128],
                                xqh[tc][:, dc * CH:(dc + 1) * CH],
                                start=(dc == 0), stop=(dc == 31), inc=(dc == 31))
                    pv = pe.ctr.v
                    ve.wait(s_pe, pv)
                    vv = ve.call("tensor_scalar",
                                 sb_axg[:, nt * TOWN + tc * CH:nt * TOWN + (tc + 1) * CH],
                                 pl[:], sb_gnr[:, tc * 4 + nt:tc * 4 + nt + 1], None,
                                 ALU.mult)
                    pe.wait(s_ve, vv)
            for tc in range(2):
                for dc in range(32):
                    pe.call("matmul", pls[:],
                            asb[:, dc * RANK:(dc + 1) * RANK],
                            xqh[tc][:, dc * CH:(dc + 1) * CH],
                            start=(dc == 0), stop=(dc == 31), inc=(dc == 31))
                pv = pe.ctr.v
                ve.wait(s_pe, pv)
                vv = ve.call("tensor_copy", sb_axs[:, tc * CH:(tc + 1) * CH], pls[:])
                pe.wait(s_ve, vv)

    with ExitStack() as ps_lnkv:
        plk = ps_lnkv.enter_context
        wsb = plk(nc.sbuf_tensor("kv_w", [128, 32 * KVH * HD], BF16))
        kcs = [plk(nc.sbuf_tensor(f"kc{i}", [128, CH], BF16)) for i in range(2)]
        ksn = [plk(nc.sbuf_tensor(f"ks{i}", [128, CH], BF16)) for i in range(2)]
        kscr = [plk(nc.sbuf_tensor(f"kscr{i}", [128, CH], BF16)) for i in range(10)]
        pk = [plk(nc.psum_tensor(f"kp{i}", [128, CH], F32)) for i in range(4)]
        wk_deps = []
        cs_deps = [None] * NCHUNK
        kstate = {"pdep": [0, 0, 0, 0], "sdep": [None, None], "rdep": 0,
                  "edep": 0, "csfree": [0, 0], "wkfree": [0] * 8}

        def wk_hook(tc):
            if tc < 2:
                for piece in range(4 * tc, 4 * tc + 4):
                    wk_deps.append(gd.dma(
                        out=wsb[:].rearrange("p (a o) -> p a o", a=32)[
                            :, piece * 4:(piece + 1) * 4, :],
                        in_=fview(i_wkT)[:, piece * 4:(piece + 1) * 4, :]))
            if tc < NCHUNK:
                ci = tc % 2
                gd.wait(s_ve, kstate["csfree"][ci])
                cs_deps[tc] = [
                    gd.dma(out=kcs[ci][:], in_=i_cos_kv[:, tc * CH:(tc + 1) * CH]),
                    gd.dma(out=ksn[ci][:], in_=i_sin_kv[:, tc * CH:(tc + 1) * CH])]

        def rope_pair(pv, half, tc, ci):
            er0, orr0, er1, orr1, t1, t2, ke0, ko0, ke1, ko1 = kscr
            ve.wait(s_pe, pv)
            ve.wait(s_ve, kstate["edep"])
            ve.call("tensor_copy", er0[:], pk[0][:])
            ve.call("tensor_copy", orr0[:], pk[1][:])
            ve.call("tensor_copy", er1[:], pk[2][:])
            c2 = ve.call("tensor_copy", orr1[:], pk[3][:])
            for g in range(4):
                kstate["pdep"][g] = c2
            ve.wait(s_ve, c2)
            cs, sn = kcs[ci][:], ksn[ci][:]
            vlast = 0
            for g2, (er, orr, ke, ko) in enumerate(
                    ((er0, orr0, ke0, ko0), (er1, orr1, ke1, ko1))):
                pr = 2 * half + g2
                ve.wait(s_ve, vlast)
                ve.call("tensor_mul", t1[:], er[:], cs)
                v2 = ve.call("scalar_tensor_tensor", t2[:], orr[:], -1.0, sn,
                             ALU.mult, ALU.mult)
                ve.wait(s_ve, v2)
                ve.waitd(kstate["sdep"][g2])
                v3 = ve.call("tensor_add", ke[:], t1[:], t2[:])
                ve.wait(s_ve, v3)
                ve.call("tensor_mul", t1[:], orr[:], cs)
                v5 = ve.call("tensor_mul", t2[:], er[:], sn)
                ve.wait(s_ve, v5)
                v6 = ve.call("tensor_add", ko[:], t1[:], t2[:])
                sy.wait(s_ve, v3)
                s1 = sy.dma(out=sc_k[pr, tc, 0], in_=ke[:])
                sy.wait(s_ve, v6)
                s2 = sy.dma(out=sc_k[pr, tc, 1], in_=ko[:])
                kstate["sdep"][g2] = [s1, s2]
                vlast = v6
            kstate["rdep"] = vlast
            kstate["edep"] = vlast

        def k_hook(tc, xch_t, ttlist):
            ci = tc % 2
            pe.waitd(wk_deps)
            ve.waitd(cs_deps[tc])
            for half in range(2):
                for dc in range(32):
                    if half == 0:
                        pe.wait(s_ve, ttlist[dc])
                    for g in range(4):
                        if dc == 0:
                            pe.wait(s_ve, kstate["pdep"][g])
                        idx = 4 * half + g
                        lastwk = (tc == NCHUNK - 1 and half == 1 and g == 3
                                  and dc % 4 == 3)
                        pe.call("matmul", pk[g][:],
                                wsb[:, dc * 1024 + idx * 128:
                                    dc * 1024 + (idx + 1) * 128],
                                xch_t[:, dc * CH:(dc + 1) * CH],
                                start=(dc == 0), stop=(dc == 31),
                                inc=(dc == 31 and g == 3) or lastwk)
                        if lastwk:
                            kstate["wkfree"][dc // 4] = pe.ctr.v
                pv = pe.ctr.v
                rope_pair(pv, half, tc, ci)
            kstate["csfree"][ci] = kstate["rdep"]
            return pe.ctr.v

        ln_pass(i_xTkv, S, sc_xkv, "kv", hook=wk_hook, post_hook=k_hook)
        xqh0 = plk(nc.sbuf_tensor("q_x0", [128, 32 * CH], BF16))
        xqh1 = plk(nc.sbuf_tensor("q_x1", [128, 32 * CH], BF16))
        wq0 = plk(nc.sbuf_tensor("q_w0", [128, 32 * 128], BF16))
        gq0 = []
        with ExitStack() as psv:
            xp = [psv.enter_context(nc.sbuf_tensor("kv_x0", [128, 32 * CH], BF16))]
            gdf = gd.frontier()
            syf = sy.frontier()
            wv_deps = []
            for piece in range(8):
                gd.wait(s_pe, kstate["wkfree"][piece])
                wv_deps.append(gd.dma(
                    out=wsb[:].rearrange("p (a o) -> p a o", a=32)[
                        :, piece * 4:(piece + 1) * 4, :],
                    in_=fview(i_wvT)[:, piece * 4:(piece + 1) * 4, :]))
            for d in sy.frontier():
                gd.wait(*d)
            gq0.append(gd.dma(out=wq0[:].rearrange("p (a o) -> p a o", a=32),
                              in_=fview(i_wqT)[:, :, 0:128]))
            xv0 = [sy.dma(
                out=xp[0][:].rearrange("p (a t) -> p a t", a=32)[
                    :, pc * 8:(pc + 1) * 8, :],
                in_=fview(sc_xkv)[:, pc * 8:(pc + 1) * 8, 0:CH])
                for pc in range(4)]
            barrier(gd_front=gdf, sy_front=syf)
            kv_sweeps(wsb, kscr, pk, xp, xqh0, xqh1, wv_deps, xv0)
        aall = plk(nc.sbuf_tensor("lp_aall", [128, 32 * NRTOT], BF16))
        asb = plk(nc.sbuf_tensor("lp_as", [128, 32 * RANK], BF16))
        lg = []
        barrier()
        q_proj(xqh0, xqh1, wq0, gq0, pk, kcs, ksn, kscr, aall, asb, lg)
        barrier()
        lora_pre(xqh0, xqh1, aall, asb, lg)
    syf_q = sy.frontier()
    barrier()


    def final_phase(kn0, bl0, bs0, fdeps):
        with ExitStack() as ps:
            pp = ps.enter_context
            xq = pp(nc.sbuf_tensor("f_x", [128, 32 * TOWN], BF16))
            ballb = [bl0, pp(nc.sbuf_tensor("f_bl1", [128, 4 * CH], BF16))]
            bsb = [bs0, pp(nc.sbuf_tensor("f_bs1", [RANK, CH], BF16))]
            pf = [pp(nc.psum_tensor(f"f_p{i}", [128, CH], F32)) for i in range(2)]
            pl = pp(nc.psum_tensor("f_pl", [128, CH], F32))
            pls = pp(nc.psum_tensor("f_pls", [RANK, CH], F32))

            xv = [[sy.dma(out=xq[:].rearrange("p (a t) -> p a t", a=32)[
                              :, pc * 8:(pc + 1) * 8, 0:CH],
                          in_=fview(sc_xkv)[:, pc * 8:(pc + 1) * 8, 3 * CH:4 * CH])
                   for pc in range(4)],
                  [sy.dma(out=xq[:].rearrange("p (a t) -> p a t", a=32)[:, :, CH:2 * CH],
                          in_=fview(sc_xkv)[:, :, 7 * CH:8 * CH])]]
            bias = pp(nc.sbuf_tensor("f_bias", [128, D], F32))
            kn = [kn0, pp(nc.sbuf_tensor("f_k1", [128, 32 * CH], BF16))]
            xc = pp(nc.sbuf_tensor("f_xc", [128, 4 * CH], F32))
            at = [pp(nc.sbuf_tensor(f"f_at{i}", [128, CH], BF16)) for i in range(2)]
            st1 = [pp(nc.sbuf_tensor(f"f_s1_{i}", [128, CH], F32)) for i in range(2)]
            bsd = gd.dma(out=bias[:], in_=i_bias)
            ve.waitd(bsd)

            pdep = [0, 0]
            atdep = [0, 0]
            kdep = [0, 0]
            bldep = [0, 0]
            s1dep = [0, 0]
            s2dep = [None, None]
            xcdep = 0
            kv_ = [fdeps, None]
            bl_ = [fdeps, None]
            bs_ = [fdeps, None]
            for oc in range(8):
                ki = oc % 2
                if oc + 1 < 8:
                    ni = (oc + 1) % 2
                    gd.wait(s_pe, kdep[ni])
                    kv_[ni] = gd.dma(
                        out=kn[ni][:].rearrange("p (a o) -> p a o", a=32),
                        in_=fview(i_kern)[:, :, (oc + 1) * CH:(oc + 2) * CH])
                    gd.wait(s_pe, bldep[ni])
                    bl_[ni] = gd.dma(
                        out=ballb[ni][:].rearrange("p (a o) -> p a o", a=4),
                        in_=i_ball.rearrange("(a p) o -> p a o", p=128)[
                            :, :, (oc + 1) * CH:(oc + 2) * CH])
                    bs_[ni] = gd.dma(out=bsb[ni][:],
                                     in_=i_bsT[:, (oc + 1) * CH:(oc + 2) * CH])
                xcv = None
                for tt in range(8):
                    if tt % 4 == 0:
                        sy.wait(s_ve, xcdep)
                        xcv = sy.dma(out=xc[:].rearrange("p (a t) -> p a t", a=4),
                                     in_=i_xown.rearrange("(a p) d -> p a d", p=128)[
                                         :, tt // 4 * 4:tt // 4 * 4 + 4,
                                         oc * CH:(oc + 1) * CH])
                    par = tt % 2
                    pb = pf[par]
                    api = tt % 2
                    sy.wait(s_ve, atdep[api])
                    av_ = sy.dma(out=at[api][:],
                                 in_=sc_at[tt * 128:(tt + 1) * 128,
                                           oc * CH:(oc + 1) * CH])
                    pe.waitd(kv_[ki])
                    pe.waitd(bl_[ki])
                    pe.waitd(bs_[ki])
                    if tt // 4 == 1 or oc > 0:
                        pe.waitd(xv[tt // 4])
                    pe.wait(s_ve, pdep[par])
                    for dc in range(32):
                        if oc == 0 and tt // 4 == 0 and dc % 8 == 0:
                            pe.waitd(xv[0][dc // 8])
                        pe.call("matmul", pb[:],
                                xq[:, dc * TOWN + (tt // 4) * CH + (tt % 4) * 128:
                                   dc * TOWN + (tt // 4) * CH + (tt % 4 + 1) * 128],
                                kn[ki][:, dc * CH:(dc + 1) * CH],
                                start=(dc == 0), stop=False, inc=False)
                    for nt in range(4):
                        pe.call("matmul", pb[:],
                                sb_axg[:, nt * TOWN + (tt // 4) * CH + (tt % 4) * 128:
                                       nt * TOWN + (tt // 4) * CH + (tt % 4 + 1) * 128],
                                ballb[ki][:, nt * CH:(nt + 1) * CH],
                                start=False, stop=False, inc=False)
                    pe.call("matmul", pb[:],
                            sb_axs[:, (tt // 4) * CH + (tt % 4) * 128:
                                   (tt // 4) * CH + (tt % 4 + 1) * 128],
                            bsb[ki][:, :],
                            start=False, stop=True)
                    pv = pe.ctr.v
                    kdep[ki] = pv
                    bldep[ki] = pv
                    ve.wait(s_pe, pv)
                    ve.waitd(av_)
                    ve.waitd(s2dep[par])
                    v1 = ve.call("tensor_add", st1[par][:], pb[:], at[api][:])
                    pdep[par] = v1
                    atdep[api] = v1
                    ve.waitd(xcv)
                    ve.wait(s_ve, v1)
                    v2 = ve.call("tensor_add", st1[par][:], st1[par][:],
                                 xc[:, (tt % 4) * CH:(tt % 4 + 1) * CH])
                    ve.wait(s_ve, v2)
                    v3 = ve.call("tensor_add", st1[par][:], st1[par][:],
                                 bias[:, oc * CH:(oc + 1) * CH])
                    xcdep = v3
                    s1dep[par] = v3
                    sy.wait(s_ve, v3)
                    sv = sy.dma(out=o_out[tt * 128:(tt + 1) * 128,
                                          oc * CH:(oc + 1) * CH],
                                in_=st1[par][:])
                    s2dep[par] = sv



    f_kn0 = e(nc.sbuf_tensor("f_k0", [128, 32 * CH], BF16))
    f_bl0 = e(nc.sbuf_tensor("f_bl0", [128, 4 * CH], BF16))
    f_bs0 = e(nc.sbuf_tensor("f_bs0", [RANK, CH], BF16))
    fdeps = []

    def f_prefetch():
        fdeps.append(gd.dma(out=f_kn0[:].rearrange("p (a o) -> p a o", a=32),
                            in_=fview(i_kern)[:, :, 0:CH]))
        fdeps.append(gd.dma(
            out=f_bl0[:].rearrange("p (a o) -> p a o", a=4),
            in_=i_ball.rearrange("(a p) o -> p a o", p=128)[:, :, 0:CH]))
        fdeps.append(gd.dma(out=f_bs0[:], in_=i_bsT[:, 0:CH]))

    with ExitStack() as ps67:
        pp = ps67.enter_context
        sb_attn = pp(nc.sbuf_tensor("p_attn", [128, NH * TOWN], BF16))
        wo = [pp(nc.sbuf_tensor(f"w_w{i}", [128, 32 * CH], BF16)) for i in range(2)]

        def attention():
            with ExitStack() as ps:
                p2 = ps.enter_context
                sb_cm4 = p2(nc.sbuf_tensor("p_cm4", [128, 4 * CH], BF16))
                gd.waitd(syf_q)
                cmld = gd.dma(out=sb_cm4[:], in_=i_cm4)
                ve.waitd(cmld)
                qsb = [p2(nc.sbuf_tensor(f"a_q{i}", [128, 2 * CH], BF16)) for i in range(2)]
                kb = [p2(nc.sbuf_tensor(f"a_k{i}", [128, CH], BF16)) for i in range(3)]
                vb = [p2(nc.sbuf_tensor(f"a_v{i}", [128, 4 * HD], BF16)) for i in range(3)]
                psb = [p2(nc.sbuf_tensor(f"a_p{i}", [128, 2 * CH], BF16)) for i in range(4)]
                prow = [p2(nc.sbuf_tensor(f"a_w{i}", [128, CH], BF16)) for i in range(4)]
                rcs0 = p2(nc.sbuf_tensor("a_rc0", [1, CH], F32))
                rcs = [rcs0, rcs0]
                bcs0 = p2(nc.sbuf_tensor("a_bc0", [128, CH], F32))
                bcs = [bcs0, bcs0]
                p_sc = [p2(nc.psum_tensor(f"a_s{i}", [128, 2 * CH], F32)) for i in range(2)]
                p_av = [p2(nc.psum_tensor(f"a_av{i}", [128, CH], F32)) for i in range(2)]
                p_rw = [p2(nc.psum_tensor(f"a_rw{i}", [128, CH], F32)) for i in range(2)]

                blocks = [(kvh, duo, qb)
                          for kvh in range(KVH) for duo in range(2) for qb in range(2)]
                sched = []
                for bi, (kvh, duo, qb) in enumerate(blocks):
                    vis = [v for v in range(12) if VISITS[v][0] == qb]
                    for vi, v in enumerate(vis):
                        sched.append((bi, vi, v))
                NG = len(sched)

                qdep = {}
                qld = {}
                kdep = [0, 0, 0]
                vdep = [0, 0, 0]
                scdep = [0, 0]
                pbdep = [0, 0, 0, 0]
                prdep = [0, 0, 0, 0]
                exv = {}
                exk = {}
                avdep = [0, 0]
                rwdep = [0, 0]
                rcsd = [0]
                bcsd = [0]
                kvst = {}

                def load_kv(kvh, pos, par):
                    gd.wait(s_pe, kdep[par])
                    kl = [gd.dma(out=kb[par][0:64, :],
                                 in_=sc_k[kvh // 2, pos, 0,
                                          64 * (kvh % 2):64 * (kvh % 2) + 64, :]),
                          gd.dma(out=kb[par][64:128, :],
                                 in_=sc_k[kvh // 2, pos, 1,
                                          64 * (kvh % 2):64 * (kvh % 2) + 64, :])]
                    gd.wait(s_pe, vdep[par])
                    vl = gd.dma(out=vb[par][:].rearrange("p (a d) -> p a d", a=4),
                                in_=sc_v[kvh, pos].rearrange("(a p) d -> p a d", p=128))
                    return kl, vl

                def load_q(bi):
                    kvh, duo, qb = blocks[bi]
                    qpar = bi % 2
                    qh = (4 * kvh + 2 * duo, 4 * kvh + 2 * duo + 1)
                    for z in range(2):
                        h = qh[z]
                        gd.wait(s_pe, qdep.get((z, qpar), 0))
                        qld[(z, qpar)] = [
                            gd.dma(out=qsb[z][0:64, qpar * CH:(qpar + 1) * CH],
                                   in_=sc_q[2 * (h // 2),
                                            64 * (h % 2):64 * (h % 2) + 64,
                                            qb * CH:(qb + 1) * CH]),
                            gd.dma(out=qsb[z][64:128, qpar * CH:(qpar + 1) * CH],
                                   in_=sc_q[2 * (h // 2) + 1,
                                            64 * (h % 2):64 * (h % 2) + 64,
                                            qb * CH:(qb + 1) * CH])]

                def scores(z, g, v, par, qpar, kl):
                    zz = 2 * z + g
                    pe.waitd(kl)
                    pe.waitd(qld[(z, qpar)])
                    pe.wait(s_ac, scdep[z])
                    for t in range(2):
                        pe.call("matmul", p_sc[z][:, t * CH:(t + 1) * CH],
                                kb[par][:, (2 * g + t) * 128:(2 * g + t + 1) * 128],
                                qsb[z][:, qpar * CH:(qpar + 1) * CH],
                                start=True, stop=True, inc=(t == 1))
                    spv = pe.ctr.v
                    qdep[(z, qpar)] = spv
                    ac.wait(s_pe, spv)
                    ac.wait(s_pe, pbdep[zz])
                    a = ac.call("activation", psb[zz][:], p_sc[z][:], AF.Exp,
                                scale=SCALE, bias=sb_btab[:, v:v + 1])
                    scdep[z] = a
                    if v in DIAG:
                        ve.wait(s_ac, a)
                        mv = 0
                        for t in range(2):
                            kt = 2 * g + t
                            mv = ve.call("tensor_mul",
                                         psb[zz][:, t * CH:(t + 1) * CH],
                                         psb[zz][:, t * CH:(t + 1) * CH],
                                         sb_cm4[:, kt * CH:(kt + 1) * CH])
                        exv[(z, g)] = mv
                        exk[(z, g)] = "ve"
                        ve.wait(s_ve, mv)
                    else:
                        exv[(z, g)] = a
                        exk[(z, g)] = "ac"
                        ve.wait(s_ac, a)
                    ve.wait(s_pe, prdep[zz])
                    pav = ve.call("tensor_add", prow[zz][:],
                                  psb[zz][:, 0:CH], psb[zz][:, CH:2 * CH])
                    exv[(z, g, "row")] = pav

                def avrows(z, g, par, vl, first, last):
                    zz = 2 * z + g
                    if exk[(z, g)] == "ve":
                        pe.wait(s_ve, exv[(z, g)])
                    else:
                        pe.wait(s_ac, exv[(z, g)])
                    pe.waitd(vl)
                    if first:
                        pe.wait(s_ve, avdep[z])
                        pe.wait(s_ve, rwdep[z])
                    for t in range(2):
                        kt = 2 * g + t
                        pe.call("matmul", p_av[z][:],
                                vb[par][:, kt * 128:(kt + 1) * 128],
                                psb[zz][:, t * CH:(t + 1) * CH],
                                start=(first and t == 0), stop=(last and t == 1),
                                inc=False)
                    pe.wait(s_ve, exv[(z, g, "row")])
                    pe.call("matmul", p_rw[z][0:1, :], sb_onesb[:, 0:1],
                            prow[zz][:], start=first, stop=last)
                    pbdep[zz] = pe.ctr.v
                    prdep[zz] = pe.ctr.v
                    kdep[par] = pe.ctr.v
                    vdep[par] = pe.ctr.v

                def finalize(bi):
                    kvh, duo, qb = blocks[bi]
                    qh = (4 * kvh + 2 * duo, 4 * kvh + 2 * duo + 1)
                    for z in range(2):
                        h = qh[z]
                        ve.wait(s_pe, pbdep[2 * z])
                        ve.wait(s_pe, pbdep[2 * z + 1])
                        ve.wait(s_pe, rcsd[0])
                        rv = ve.call("reciprocal", rcs[z][0:1, :], p_rw[z][0:1, :])
                        rwdep[z] = rv
                        pe.wait(s_ve, rv)
                        bv = pe.call("matmul", p_rw[z][:, :], sb_onesf[0:1, :],
                                     rcs[z][0:1, :], start=True, stop=True)
                        rcsd[0] = bv
                        ve.wait(s_pe, bv)
                        ve.wait(s_ve, bcsd[0])
                        cv = ve.call("tensor_copy", bcs[z][:], p_rw[z][:, :])
                        rwdep[z] = cv
                        ve.wait(s_ve, cv)
                        nv2 = ve.call("tensor_mul",
                                      sb_attn[:, h * TOWN + qb * CH:
                                              h * TOWN + (qb + 1) * CH],
                                      p_av[z][:], bcs[z][:])
                        avdep[z] = nv2
                        bcsd[0] = nv2

                load_q(0)
                kvst[0] = load_kv(blocks[0][0], VISITS[sched[0][2]][1], 0)
                for gi, (bi, vi, v) in enumerate(sched):
                    qpar = bi % 2
                    par = gi % 3
                    kl, vl = kvst[gi]
                    if gi + 1 < NG:
                        nbi, nvi, nv_ = sched[gi + 1]
                        kvst[gi + 1] = load_kv(blocks[nbi][0], VISITS[nv_][1],
                                               (gi + 1) % 3)
                        if nvi == 0:
                            load_q(nbi)
                    scores(0, 0, v, par, qpar, kl)
                    scores(1, 0, v, par, qpar, kl)
                    if gi > 0:
                        pvl = kvst[gi - 1][1]
                        ppar = (gi - 1) % 3
                        avrows(0, 1, ppar, pvl, False, vi == 0)
                        avrows(1, 1, ppar, pvl, False, vi == 0)
                    if vi == 0 and bi > 0:
                        finalize(bi - 1)
                    scores(0, 1, v, par, qpar, kl)
                    scores(1, 1, v, par, qpar, kl)
                    avrows(0, 0, par, vl, vi == 0, False)
                    avrows(1, 0, par, vl, vi == 0, False)
                gl = NG - 1
                avrows(0, 1, gl % 3, kvst[gl][1], False, True)
                avrows(1, 1, gl % 3, kvst[gl][1], False, True)
                finalize(len(blocks) - 1)

        attention()
        gdf = gd.frontier()
        gv0 = gd.dma(out=wo[0][:].rearrange("p (a o) -> p a o", a=32),
                     in_=fview(i_woT)[:, :, 0:CH])
        barrier(gd_front=gdf)

        def wo_phase(gv0, pf_hook):
            with ExitStack() as ps:
                p2 = ps.enter_context
                stg = [p2(nc.sbuf_tensor(f"w_s{i}", [128, CH], BF16)) for i in range(2)]
                pw = [p2(nc.psum_tensor(f"w_p{i}", [128, CH], F32)) for i in range(2)]
                wdep = [0, 0]
                pdep = [0, 0]
                sdep = [None, None]
                gv = [gv0, None]
                for dc in range(8):
                    wi = dc % 2
                    if dc + 1 < 8:
                        ni = (dc + 1) % 2
                        gd.wait(s_pe, wdep[ni])
                        gv[ni] = gd.dma(
                            out=wo[ni][:].rearrange("p (a o) -> p a o", a=32),
                            in_=fview(i_woT)[:, :, (dc + 1) * CH:(dc + 2) * CH])
                    if dc == 2:
                        pf_hook()
                    pe.waitd(gv[wi])
                    for tt in range(8):
                        par = tt % 2
                        pb = pw[par]
                        pe.wait(s_ve, pdep[par])
                        for oc in range(32):
                            qb, tl = tt // 4, tt % 4
                            pe.call("matmul", pb[:],
                                    sb_attn[:, oc * TOWN + qb * CH + tl * 128:
                                            oc * TOWN + qb * CH + (tl + 1) * 128],
                                    wo[wi][:, oc * CH:(oc + 1) * CH],
                                    start=(oc == 0), stop=(oc == 31), inc=(oc == 31))
                        pv = pe.ctr.v
                        wdep[wi] = pv
                        ve.wait(s_pe, pv)
                        ve.waitd(sdep[par])
                        cv = ve.call("tensor_copy", stg[par][:], pb[:])
                        pdep[par] = cv
                        sy.wait(s_ve, cv)
                        sv = sy.dma(out=sc_at[tt * 128:(tt + 1) * 128,
                                              dc * CH:(dc + 1) * CH],
                                    in_=stg[par][:])
                        sdep[par] = sv

        wo_phase(gv0, f_prefetch)
    barrier()
    final_phase(f_kn0, f_bl0, f_bs0, fdeps)
    for sem, v in sy.frontier():
        sy.wait(sem, v)

    with nc.Block() as block:

        @block.gpsimd
        def _(g):
            for f in gd.plan:
                f(g)

        @block.sync
        def _(s):
            for f in sy.plan:
                f(s)

        @block.tensor
        def _(t):
            for f in pe.plan:
                f(t)

        @block.vector
        def _(v):
            for f in ve.plan:
                f(v)

        @block.scalar
        def _(s):
            for f in ac.plan:
                f(s)

    stack.close()
    return nc


def _rope_tables():
    inv = 1.0 / (500000.0 ** (np.arange(0, HD, 2, dtype=np.float64) / HD))
    t = np.arange(S, dtype=np.float64)
    ang = np.outer(inv, t)
    cos = np.cos(ang).astype(np.float32)
    sin = np.sin(ang).astype(np.float32)
    cos2 = np.concatenate([cos, cos], 0)
    sin2 = np.concatenate([sin, sin], 0)
    return cos2, sin2


def _half_perm(nheads):
    perm = []
    for p in range(nheads // 2):
        for half in range(2):
            for h in (2 * p, 2 * p + 1):
                base = h * HD + half * 64
                perm.extend(range(base, base + 64))
    return np.asarray(perm)


_NC_CACHE = {}


def prepare_in_maps(L0_attn_x, L0_attn_w_qkv, L0_attn_w_o, L0_ffn_gate,
                    L0_ffn_kernel, L0_ffn_bias, L0_ffn_lora_A, L0_ffn_lora_B,
                    L0_ffn_lora_A_shared, L0_ffn_lora_B_shared):
    x = np.ascontiguousarray(np.asarray(L0_attn_x, dtype=np.float32))
    w_qkv = np.asarray(L0_attn_w_qkv, dtype=np.float32)
    w_o = np.asarray(L0_attn_w_o, dtype=np.float32)
    gate = np.asarray(L0_ffn_gate, dtype=np.float32)
    kern = np.asarray(L0_ffn_kernel, dtype=np.float32)
    biasv = np.asarray(L0_ffn_bias, dtype=np.float32)
    lA = np.asarray(L0_ffn_lora_A, dtype=np.float32)
    lB = np.asarray(L0_ffn_lora_B, dtype=np.float32)
    lAs = np.asarray(L0_ffn_lora_A_shared, dtype=np.float32)
    lBs = np.asarray(L0_ffn_lora_B_shared, dtype=np.float32)

    permq = _half_perm(NH)
    permk = _half_perm(KVH)
    wqT = np.ascontiguousarray(w_qkv[:D].T[:, permq])
    wkT = np.ascontiguousarray(w_qkv[D:D + KVH * HD].T[:, permk])
    wvT = np.ascontiguousarray(w_qkv[D + KVH * HD:].T)
    woT = np.ascontiguousarray(w_o.T)
    biasT = np.ascontiguousarray(np.broadcast_to(biasv, (128, D)))
    aallT = np.ascontiguousarray(lA.transpose(2, 0, 1).reshape(D, NRTOT))
    ball = np.ascontiguousarray(lB.transpose(0, 2, 1).reshape(NRTOT, D))
    asT = np.ascontiguousarray(lAs.T)
    bsT = np.ascontiguousarray(lBs.T)
    cos2, sin2 = _rope_tables()

    cm4 = np.zeros((128, 4 * CH), dtype=bfdt)
    for kt in range(4):
        kvloc = kt * 128 + np.arange(128)[:, None]
        cm4[:, kt * CH:(kt + 1) * CH] = (kvloc <= np.arange(CH)[None, :]).astype(bfdt)
    onesf = np.ones((1, 128), np.float32)
    onesb = np.ones((128, 1), bfdt)

    in_maps = []
    slices = []
    for c in range(8):
        b, j = c // 4, c % 4
        A_real, B_real = j, 7 - j
        place = [-1] * NCHUNK
        place[3] = A_real
        place[7] = B_real
        for i in range(j):
            place[i] = i
        rest = [cchunk for cchunk in range(j + 1, 7 - j)]
        free = [p for p in range(7) if place[p] < 0]
        for p, cchunk in zip(free, rest):
            place[p] = cchunk
        leftover = [cchunk for cchunk in range(NCHUNK) if cchunk not in place]
        li = 0
        for p in range(NCHUNK):
            if place[p] < 0:
                place[p] = leftover[li]
                li += 1
        kvtok = np.concatenate([np.arange(place[p] * CH, (place[p] + 1) * CH)
                                for p in range(NCHUNK)])
        owntok = np.concatenate([np.arange(A_real * CH, (A_real + 1) * CH),
                                 np.arange(B_real * CH, (B_real + 1) * CH)])
        xb = x[b].reshape(S, D)
        xT = xb.T
        xTkv = np.ascontiguousarray(xT[:, kvtok])
        xown = np.ascontiguousarray(xb[owntok])

        btab = np.zeros((128, 12), np.float32)
        for v, (qb, p) in enumerate(VISITS):
            if v in DIAG:
                btab[:, v] = -SHIFT
            else:
                qc = A_real if qb == 0 else B_real
                btab[:, v] = -SHIFT if place[p] < qc else EMPTY

        lbA = b * 4 + A_real // 2
        lbB = b * 4 + B_real // 2
        gnr = np.zeros((128, 8), np.float32)
        p_idx = np.arange(128)
        for tc, lb in ((0, lbA), (1, lbB)):
            for nt in range(4):
                gnr[:, tc * 4 + nt] = gate[lb, 4 * nt + p_idx // 32]

        in_maps.append({
            "xTkv": xTkv, "xown": xown,
            "wqT": wqT, "wkT": wkT, "wvT": wvT, "woT": woT,
            "kern": kern, "biasT": biasT,
            "aallT": aallT, "ball": ball, "asT": asT, "bsT": bsT,
            "gatenr": gnr,
            "cos_kv": np.ascontiguousarray(cos2[:, kvtok]),
            "sin_kv": np.ascontiguousarray(sin2[:, kvtok]),
            "cos_q": np.ascontiguousarray(cos2[:, owntok]),
            "sin_q": np.ascontiguousarray(sin2[:, owntok]),
            "cmask4": cm4, "btab": btab,
            "onesf": onesf, "onesb": onesb, "onesbr": np.ones((1, 128), bfdt),
            "epsc": np.full((128, 1), EPS, np.float32),
        })
        slices.append((b, owntok))
    return in_maps, slices


def kernel(**inputs):
    in_maps, slices = prepare_in_maps(**inputs)
    if "nc" not in _NC_CACHE:
        _NC_CACHE["nc"] = build_program()
    nc = _NC_CACHE["nc"]
    res = run_bass_kernel_spmd(nc, in_maps, core_ids=list(range(8)))
    out = np.empty((2, S, D), dtype=np.float32)
    for c in range(8):
        b, owntok = slices[c]
        out[b, owntok, :] = res.results[c]["out"]
    return out.reshape(2, S, D)


